# version 12
# baseline (speedup 1.0000x reference)
import numpy as np
import ml_dtypes
import concourse.bass as bass
import concourse.mybir as mybir
from concourse.bass_utils import run_bass_kernel_spmd
from contextlib import ExitStack

F32 = mybir.dt.float32
BF16 = mybir.dt.bfloat16
U8 = mybir.dt.uint8
AF = mybir.ActivationFunctionType
ALU = mybir.AluOpType
AX = mybir.AxisListType

NCORE = 8
S = 16384
TC = 2048
D = 1024
NH = 16
DH = 64
NPAIR = 8
NCH = 16
NT = 16
NEXP = 16
DI = 512
BIGV = 1.0e4

ENGINES = ("tensor", "vector", "scalar", "sync", "gpsimd")
CSEM = {"tensor": "st", "vector": "sv", "scalar": "ss", "sync": "sp", "gpsimd": "sg"}
QPOOL = {"sync": 3, "gpsimd": 8}


class Prog:

    def __init__(self):
        self.stream = {e: [] for e in ENGINES}
        self.cnt = {}
        self.reg = {}
        self.waited = {e: {} for e in ENGINES}
        self.iv = {}
        self.overlaps = {}
        self.n_ops = 0
        self.n_waits = 0
        self.scope = "init"

    def interval(self, name, space, lo, hi):
        self.iv[name] = (space, lo, hi)
        self.overlaps = {}

    def _family(self, name):
        got = self.overlaps.get(name)
        if got is not None:
            return got
        if name not in self.iv:
            fam = (name,)
        else:
            sp, lo, hi = self.iv[name]
            fam = tuple(
                o for o, (sp2, lo2, hi2) in self.iv.items()
                if sp2 == sp and lo < hi2 and lo2 < hi
            )
        self.overlaps[name] = fam
        return fam

    def emit(self, eng, fn, reads=(), writes=(), sem=None, amount=1):
        if sem is None:
            sem = CSEM[eng]
        waits = {}

        def need(m):
            if m is None:
                return
            s, v = m
            if s == CSEM[eng] and eng == "tensor":
                return
            if self.waited[eng].get(s, 0) >= v:
                return
            waits[s] = max(waits.get(s, 0), v)

        rfam, wfam = set(), set()
        for r in reads:
            rfam.update(self._family(r))
        for r in writes:
            wfam.update(self._family(r))
        for r in rfam:
            e = self.reg.get(r)
            if e:
                need(e["w"])
        for r in wfam:
            e = self.reg.get(r)
            if e:
                need(e["w"])
                for s, v in e["r"].items():
                    need((s, v))
        for s, v in sorted(waits.items()):
            self.stream[eng].append(("wait", s, v))
            self.waited[eng][s] = max(self.waited[eng].get(s, 0), v)
            self.n_waits += 1
        self.cnt[sem] = self.cnt.get(sem, 0) + amount
        v_after = self.cnt[sem]
        self.stream[eng].append(("op", fn, sem, amount, self.scope))
        self.n_ops += 1
        for r in rfam:
            self.reg.setdefault(r, {"w": None, "r": {}})["r"][sem] = v_after
        for r in writes:
            for rr in self._family(r):
                if rr == r:
                    self.reg[r] = {"w": (sem, v_after), "r": {}}
                else:
                    self.reg[rr] = {"w": (sem, v_after), "r": {}}

    def dma(self, eng, fn, reads=(), writes=()):
        pool = QPOOL[eng]
        i = self.cnt.get("dmaidx_" + eng, 0)
        self.cnt["dmaidx_" + eng] = i + 1
        sem = "q%s%d" % (eng[0], i % pool)
        gate = 16 * (i // pool)
        if gate > 0 and self.waited[eng].get(sem, 0) < gate:
            self.stream[eng].append(("wait", sem, gate))
            self.waited[eng][sem] = gate
            self.n_waits += 1
        self.emit(eng, fn, reads, writes, sem=sem, amount=16)


def build_program(upto=99, debug=False, c_pairs=NPAIR, c_step=9):
    nc = bass.Bass("TRN2", target_bir_lowering=False)

    x_d = nc.dram_tensor("x", [TC, D], F32, kind="ExternalInput").ap()
    wqT_d = nc.dram_tensor("wqT", [NPAIR, 128, D], F32, kind="ExternalInput").ap()
    wkT_d = nc.dram_tensor("wkT", [NPAIR, 128, D], F32, kind="ExternalInput").ap()
    wvT_d = nc.dram_tensor("wvT", [NPAIR, 128, D], F32, kind="ExternalInput").ap()
    woT_d = nc.dram_tensor("woT", [128, 8 * D], F32, kind="ExternalInput").ap()
    bq_d = nc.dram_tensor("bq", [1, D], F32, kind="ExternalInput").ap()
    bk_d = nc.dram_tensor("bk", [1, D], F32, kind="ExternalInput").ap()
    bv_d = nc.dram_tensor("bv", [1, D], F32, kind="ExternalInput").ap()
    bo_d = nc.dram_tensor("bo", [1, D], F32, kind="ExternalInput").ap()
    gwT_d = nc.dram_tensor("gwT", [128, 8 * NEXP], F32, kind="ExternalInput").ap()
    ebias_d = nc.dram_tensor("ebias", [128, NEXP], F32, kind="ExternalInput").ap()
    gateT_d = nc.dram_tensor("gateT", [NEXP, 128, 8 * DI], F32, kind="ExternalInput").ap()
    upT_d = nc.dram_tensor("upT", [NEXP, 128, 8 * DI], F32, kind="ExternalInput").ap()
    downT_d = nc.dram_tensor("downT", [NEXP, 128, 4 * D], F32, kind="ExternalInput").ap()
    idb_d = nc.dram_tensor("idb", [128, 128], BF16, kind="ExternalInput").ap()
    mask2_d = nc.dram_tensor("mask2", [128, 512], F32, kind="ExternalInput").ap()
    iotmb_d = nc.dram_tensor("iotmb", [128, 256], F32, kind="ExternalInput").ap()
    pmask_d = nc.dram_tensor("pmask", [128, NCORE], F32, kind="ExternalInput").ap()
    out_d = nc.dram_tensor("out", [TC, D], F32, kind="ExternalOutput").ap()

    cc_in = nc.dram_tensor("cc_in", [64, NH * 65], F32)
    sloc_d = nc.dram_tensor("sloc_d", [NPAIR, 128, NCH * 65], BF16)
    cc_out = nc.dram_tensor("cc_out", [64 * NCORE, NH * 65], F32)
    if debug:
        xf_dbg = nc.dram_tensor("xf_dbg", [128, 16384], BF16, kind="ExternalOutput").ap()
        ccin_dbg = nc.dram_tensor("ccin_dbg", [64, NH * 65], F32, kind="ExternalOutput").ap()
        sprep_dbg = nc.dram_tensor("sprep_dbg", [128, 520], F32, kind="ExternalOutput").ap()
        af_dbg = nc.dram_tensor("af_dbg", [128, 16384], BF16, kind="ExternalOutput").ap()
        x1_dbg = nc.dram_tensor("x1_dbg", [128, 16384], BF16, kind="ExternalOutput").ap()
        comb_dbg = nc.dram_tensor("comb_dbg", [128, 256], F32, kind="ExternalOutput").ap()

    p = Prog()
    es = ExitStack()

    SBUF_BYTES = 212736
    arena = es.enter_context(nc.sbuf_tensor("arena", [128, SBUF_BYTES], U8))
    alloc_off = [0]

    def carve(name, nbytes, align=64):
        off = (alloc_off[0] + align - 1) // align * align
        alloc_off[0] = off + nbytes
        assert alloc_off[0] <= SBUF_BYTES, f"SBUF overflow at {name}: {alloc_off[0]}"
        return name, off

    def view(buf, dtype, cols, part=128, poff=0):
        name, off = buf
        isz = 2 if dtype in (BF16,) else (4 if dtype in (F32,) else 1)
        apv = arena[:].bitcast(dtype)
        ap2 = apv[poff:poff + part, off // isz: off // isz + cols]
        return ap2

    def reg_iv(buf, nbytes, name=None):
        nm, off = buf
        p.interval(name or nm, "sbuf", off, off + nbytes)

    b_xf = carve("xf", 32768); reg_iv(b_xf, 32768)
    b_xbf = carve("xbf", 32768); reg_iv(b_xbf, 32768)
    b_gs = carve("gs", 2 * 2048)
    for _k in range(2):
        p.interval("gs%d" % _k, "sbuf", b_gs[1] + 2048 * _k, b_gs[1] + 2048 * (_k + 1))
    b_idb = carve("idb_s", 256); reg_iv(b_idb, 256)
    b_pm = carve("pm_s", 32); reg_iv(b_pm, 32)
    b_ones = carve("ones_s", 1024); reg_iv(b_ones, 1024)
    b_eps = carve("eps_s", 4); reg_iv(b_eps, 4)
    b_nbig = carve("nbig_s", 4); reg_iv(b_nbig, 4)
    b_tiny = carve("tiny_s", 4); reg_iv(b_tiny, 4)
    b_iotmb = carve("iotmb_s", 1024); reg_iv(b_iotmb, 1024)
    b_gw = carve("gw", 256); reg_iv(b_gw, 256)
    b_ebias = carve("ebias_s", 64); reg_iv(b_ebias, 64)

    AOFF = (alloc_off[0] + 63) // 64 * 64
    U = lambda off: AOFF + off

    def ubuf(name, off, nbytes):
        b = (name, U(off))
        p.interval(name, "sbuf", U(off), U(off) + nbytes)
        return b

    b_af = ubuf("af", 0, 32768)
    b_Kf8 = [ubuf("Kf8_%d" % _k, 32768 + 4096 * _k, 4096) for _k in range(8)]
    b_Vaug8 = [ubuf("Va8_%d" % _k, 65536 + 4160 * _k, 4160) for _k in range(8)]
    b_wring = (None, U(98816))
    for _k in range(4):
        p.interval("wr%d" % _k, "sbuf", U(98816) + 2048 * _k, U(98816) + 2048 * (_k + 1))
    b_bring = (None, U(107008))
    for _k in range(4):
        p.interval("br%d" % _k, "sbuf", U(107008) + 256 * _k, U(107008) + 256 * (_k + 1))
    b_Qf = ubuf("Qf", 108032, 4096)
    b_Qm = ubuf("Qm", 112128, 4096)
    b_scanA = ("scanA", U(108032))
    p.interval("scanA", "sbuf", U(108032), U(108032) + 2210)
    b_scanB = ("scanB", U(120320))
    p.interval("scanB", "sbuf", U(120320), U(120320) + 2080)
    b_Ktok = ubuf("Ktok", 116224, 4096)
    b_SlocS = ubuf("SlocS", 120320, 2080)
    b_SlocR = (None, U(122400))
    for _k in range(2):
        p.interval("SlR%d" % _k, "sbuf", U(122400) + 2080 * _k, U(122400) + 2080 * (_k + 1))
    b_Saug = (None, U(126560))
    for _k in range(16):
        p.interval("Sg%d" % _k, "sbuf", U(126560) + 130 * _k, U(126560) + 130 * (_k + 1))
    b_Sfp = ubuf("Sfp", 128640, 260)
    b_Mp = (None, U(128960))
    for _k in range(2):
        p.interval("Mp%d" % _k, "sbuf", U(128960) + 1024 * _k, U(128960) + 1024 * (_k + 1))
    b_otb = (None, U(131008))
    for _k in range(2):
        p.interval("otb%d" % _k, "sbuf", U(131008) + 512 * _k, U(131008) + 512 * (_k + 1))
    b_rden = (None, U(132032))
    for _k in range(2):
        p.interval("rden%d" % _k, "sbuf", U(132032) + 16 * _k, U(132032) + 16 * (_k + 1))
    b_Sprep = ubuf("Sprep", 132096, 2080)
    b_Sprepb = ubuf("Sprepb", 134176, 1040)
    b_gath = ubuf("gath", 135232, 2080)
    b_mask2 = ubuf("mask2_s", 137344, 2048)
    UNEED_ATTN = 139392

    b_er = [ubuf("er%d" % _k, 8192 * _k, 8192) for _k in range(8)]
    b_wo = ubuf("wo", 32768, 16384)
    b_act = (None, U(65536))
    for _ic in range(4):
        for _u in range(2):
            p.interval("act%d_%d" % (_ic, _u), "sbuf",
                       U(65536) + 2048 * _ic + 1024 * _u,
                       U(65536) + 2048 * _ic + 1024 * (_u + 1))
    b_yv = ubuf("yv", 73728, 32768)
    b_xsq = (None, U(106496))
    for _k in range(2):
        p.interval("xsq%d" % _k, "sbuf", U(106496) + 4096 * _k, U(106496) + 4096 * (_k + 1))
    b_xsqB = ubuf("xsqB", 114688, 4096)
    b_xt32 = (None, U(118784))
    for _k in range(2):
        p.interval("xt%d" % _k, "sbuf", U(118784) + 4096 * _k, U(118784) + 4096 * (_k + 1))
    b_mst = ubuf("mst", 126976, 768)
    b_scores = ubuf("scores", 127744, 1024)
    b_comb = ubuf("comb", 128768, 1024)
    b_tmpr = ubuf("tmpr", 129792, 1024)
    b_eq = ubuf("eqb", 130816, 1024)
    b_gall = ubuf("gall", 131840, 256)
    b_tau = ubuf("tau", 132096, 64)
    b_gm = ubuf("gm", 132160, 256)
    b_rsc = ubuf("rsc", 132416, 128)
    b_bo = ubuf("bo_s", 135232, 2048)
    UNEED = UNEED_ATTN
    alloc_off[0] = AOFF + UNEED
    assert alloc_off[0] <= SBUF_BYTES, alloc_off[0]

    xf = view(b_xf, BF16, 16384)
    xbf = view(b_xbf, BF16, 16384)
    af = view(b_af, BF16, 16384)
    yv = view(b_yv, F32, 8192)
    wo_s = view(b_wo, BF16, 8192)
    Qf = view(b_Qf, BF16, 2048)
    Qm = view(b_Qm, BF16, 2048)
    scanA = view(b_scanA, BF16, 1105)
    scanB = view((None, U(120320)), BF16, 1040)
    Kf8 = [view(b_Kf8[_k], BF16, 2048) for _k in range(8)]
    Vaug8 = [view(b_Vaug8[_k], BF16, 2080) for _k in range(8)]
    Ktok = view(b_Ktok, BF16, 2048)
    SlocS = view(b_SlocS, BF16, 1040)
    SlocR = [view((None, U(122400) + 2080 * _k), BF16, 1040) for _k in range(2)]
    Mp2 = [view((None, U(128960) + 1024 * _k), BF16, 512) for _k in range(2)]
    otb2 = [view((None, U(131008) + 512 * _k), BF16, 256) for _k in range(2)]
    Saug = view((None, U(126560)), BF16, 1040)
    Sfp = view(b_Sfp, F32, 65)
    rden2 = [view((None, U(132032) + 16 * _k), F32, 4) for _k in range(2)]
    wring = [view((None, U(98816) + 2048 * _k), BF16, 1024) for _k in range(4)]
    bring = [view((None, U(107008) + 256 * _k), BF16, 128, part=1) for _k in range(4)]
    gs2 = [view((b_gs[0], b_gs[1] + 2048 * _k), F32, 512) for _k in range(2)]
    ers = [view(b_er[_k], BF16, 4096) for _k in range(8)]
    actb = view(b_act, BF16, 4096)
    Sprep = view(b_Sprep, F32, 520)
    Sprepb = view(b_Sprepb, BF16, 520)
    gath = view(b_gath, F32, 520)
    scores = view(b_scores, F32, 256)
    comb = view(b_comb, F32, 256)
    tmpr = view(b_tmpr, F32, 256)
    eqb = view(b_eq, F32, 256)
    gall = view(b_gall, F32, 64)
    tau = view(b_tau, F32, 16)
    gm = view(b_gm, F32, 64)
    rsc = view(b_rsc, F32, 32)
    mst = view(b_mst, F32, 192)
    xt32 = [view((None, U(118784) + 4096 * _k), F32, 1024) for _k in range(2)]
    xsq2 = [view((None, U(106496) + 4096 * _k), F32, 1024) for _k in range(2)]
    xsqB = view(b_xsqB, F32, 1024)
    gw_s = view(b_gw, BF16, 128)
    bo_s = view(b_bo, BF16, 1024, part=1)
    idb = view(b_idb, BF16, 128)
    mask2 = view(b_mask2, F32, 512)
    iotmb = view(b_iotmb, F32, 256)
    pm_s = view(b_pm, F32, 8)
    ones_bf = view(b_ones, BF16, 512, part=1)
    eps_s = view(b_eps, F32, 1)
    nbig_s = view(b_nbig, F32, 1)
    tiny_s = view(b_tiny, F32, 1)
    ebias_s = view(b_ebias, F32, 16)

    pb = [es.enter_context(nc.psum_tensor("pbank%d" % _k, [128, 512], F32)) for _k in range(8)]
    for k in range(8):
        p.interval("pb%d" % k, "psum", 2048 * k, 2048 * (k + 1))

    def pbf(k):
        return pb[k][:]

    def pbb(k):
        return pb[k][:].bitcast(BF16)

    E = p.emit
    DMA = p.dma

    def EG(eng, fns, reads=(), writes=()):
        def run(e, fns=tuple(fns)):
            inst = None
            for f in fns:
                inst = f(e)
            return inst
        E(eng, run, reads=reads, writes=writes)

    def _finalize():
        sem_names = sorted({s for e in ENGINES for item in p.stream[e]
                            for s in [item[1] if item[0] == "wait" else item[2]]})
        handles = {}
        for s in sem_names:
            handles[s] = es.enter_context(nc.semaphore(s))
        qfinal = {s: v for s, v in p.cnt.items() if s.startswith("q") and not s.startswith("dmaidx")}
        with nc.Block() as block:
            def run_stream(engine, name):
                cur = None
                cur_id = None
                for item in p.stream[name]:
                    if item[0] == "wait":
                        engine.wait_ge(handles[item[1]], item[2])
                    else:
                        _, fn, sem, amount, scope = item
                        if scope != cur:
                            if cur is not None:
                                nc.leave_named_scope(cur, cur_id, False)
                            cur_id, _ = nc.enter_named_scope(scope, False)
                            cur = scope
                        inst = fn(engine)
                        inst.then_inc(handles[sem], amount)
                if cur is not None:
                    nc.leave_named_scope(cur, cur_id, False)
                if name == "sync":
                    for s, v in sorted(qfinal.items()):
                        engine.wait_ge(handles[s], v)

            @block.sync
            def _(e):
                run_stream(e, "sync")

            @block.gpsimd
            def _(e):
                run_stream(e, "gpsimd")

            @block.tensor
            def _(e):
                run_stream(e, "tensor")

            @block.vector
            def _(e):
                run_stream(e, "vector")

            @block.scalar
            def _(e):
                run_stream(e, "scalar")
        return nc, es, p

    p.scope = "load"
    DMA("sync", lambda e: e.dma_start(out=idb, in_=idb_d[:]), writes=["idb_s"])
    DMA("sync", lambda e: e.dma_start(out=mask2, in_=mask2_d[:]), writes=["mask2_s"])
    DMA("sync", lambda e: e.dma_start(out=iotmb, in_=iotmb_d[:]), writes=["iotmb_s"])
    DMA("sync", lambda e: e.dma_start(out=pm_s, in_=pmask_d[:]), writes=["pm_s"])
    DMA("sync", lambda e: e.dma_start(out=ebias_s, in_=ebias_d[:]), writes=["ebias_s"])
    DMA("gpsimd", lambda e: e.dma_start(out=xbf.rearrange("q (n h) -> q n h", h=D),
                                        in_=x_d.rearrange("(n q) h -> q n h", q=128)), writes=["xbf"])
    DMA("gpsimd", lambda e: e.dma_start(out=gw_s, in_=gwT_d), writes=["gw"])
    E("vector", lambda e: e.memset(ones_bf, 1.0), writes=["ones_s"])
    E("vector", lambda e: e.memset(eps_s, 1e-5), writes=["eps_s"])
    E("vector", lambda e: e.memset(nbig_s, -BIGV), writes=["nbig_s"])
    E("vector", lambda e: e.memset(tiny_s, 1e-20), writes=["tiny_s"])
    for pr in range(NPAIR):
        E("vector", lambda e, pr=pr: e.memset(
            Vaug8[pr].rearrange("q (j h c) -> q j h c", h=2, c=65)[:, :, :, 64:65], 1.0),
            writes=["Va8_%d" % pr])
    E("vector", lambda e: e.memset(Qm[0:64, :], 0.0), writes=["Qm"])
    E("vector", lambda e: e.memset(scanA[:, 0:65], 0.0), writes=["scanA"])

    def tile_transposes(n, src_reg):
        for c2 in range(4):
            k = 4 + ((n * 4 + c2) % 2)
            EG("tensor", [
                lambda e, n=n, c2=c2, k=k: e.transpose(
                    pbb(k)[:, 0:128], xbf[:, 1024 * n + 256 * c2: 1024 * n + 256 * c2 + 128], idb),
                lambda e, n=n, c2=c2, k=k: e.transpose(
                    pbb(k)[:, 128:256], xbf[:, 1024 * n + 256 * c2 + 128: 1024 * n + 256 * c2 + 256], idb)],
                reads=[src_reg, "idb_s"], writes=["pb%d" % k])
            E("vector", lambda e, n=n, c2=c2, k=k: e.tensor_copy(
                xf.rearrange("q (c r) -> q c r", r=2048)[:, 2 * c2:2 * c2 + 2, 128 * n:128 * (n + 1)],
                pbb(k)[:, 0:256].rearrange("q (a r) -> q a r", a=2)),
                reads=["pb%d" % k], writes=["xf"])

    p.scope = "xT"
    for n in range(NT):
        tile_transposes(n, "xbf")

    if debug:
        for n in range(16):
            DMA("sync", lambda e, n=n: e.dma_start(out=xf_dbg[:, 1024 * n:1024 * (n + 1)],
                                                   in_=xf[:, 1024 * n:1024 * (n + 1)]), reads=["xf"])
    if upto <= 1:
        return _finalize()

    wun = [0]

    def load_wslice(wd, bd, pair):
        u = wun[0]; wun[0] += 1
        slot = u % 4
        DMA("gpsimd", lambda e, wd=wd, pair=pair, slot=slot: e.dma_start(
            out=wring[slot], in_=wd[pair]),
            writes=["wr%d" % slot])
        DMA("gpsimd", lambda e, bd=bd, pair=pair, slot=slot: e.dma_start(
            out=bring[slot], in_=bd[:, 128 * pair:128 * (pair + 1)]),
            writes=["br%d" % slot])
        return slot

    def qkv_fmajor(slot, dst, dst_reg, softplus):
        for u in range(4):
            bank = u
            fns = [lambda e, u=u, c=c, slot=slot, bank=bank: e.matmul(
                pbf(bank), wring[slot].rearrange("q (c o) -> q c o", o=128)[:, c, :],
                xf[:, 2048 * c + 512 * u: 2048 * c + 512 * (u + 1)],
                start=(c == 0), stop=False) for c in range(8)]
            fns.append(lambda e, bank=bank, slot=slot: e.matmul(
                pbf(bank), bring[slot], ones_bf, start=False, stop=True))
            EG("tensor", fns,
               reads=["wr%d" % slot, "xf", "br%d" % slot, "ones_s"], writes=["pb%d" % bank])
            if softplus:
                E("scalar", lambda e, u=u, bank=bank, dst=dst: e.activation(
                    gs2[bank % 2], pbf(bank), AF.Exp),
                    reads=["pb%d" % bank], writes=["gs%d" % (bank % 2)])
                E("scalar", lambda e, u=u, bank=bank, dst=dst, dst_reg=dst_reg: e.activation(
                    dst[:, 512 * u:512 * (u + 1)], gs2[bank % 2], AF.Ln, bias=1.0),
                    reads=["gs%d" % (bank % 2)], writes=[dst_reg])
            else:
                E("vector", lambda e, u=u, bank=bank, dst=dst, dst_reg=dst_reg: e.tensor_copy(
                    dst[:, 512 * u:512 * (u + 1)], pbf(bank)),
                    reads=["pb%d" % bank], writes=[dst_reg])

    for pair in range(NPAIR):
        p.scope = "A%d" % pair
        sk = load_wslice(wkT_d, bk_d, pair)
        sv = load_wslice(wvT_d, bv_d, pair)
        qkv_fmajor(sk, Kf8[pair], "Kf8_%d" % pair, softplus=True)
        for jj in range(NCH // 2):
            k = 4 + (jj % 4)
            EG("tensor", [
                lambda e, jj=jj, k=k, pair=pair: e.transpose(
                    pbb(k)[:, 0:128], Kf8[pair][:, 256 * jj:256 * jj + 128], idb),
                lambda e, jj=jj, k=k, pair=pair: e.transpose(
                    pbb(k)[:, 128:256], Kf8[pair][:, 256 * jj + 128:256 * jj + 256], idb)],
                reads=["Kf8_%d" % pair, "idb_s"], writes=["pb%d" % k])
            E("vector", lambda e, jj=jj, k=k: e.tensor_copy(
                Ktok[:, 256 * jj:256 * (jj + 1)], pbb(k)[:, 0:256]),
                reads=["pb%d" % k], writes=["Ktok"])
        for jj in range(NCH // 2):
            bank = jj % 4
            fns = []
            for sub in range(2):
                j = 2 * jj + sub
                fns += [lambda e, j=j, c=c, sub=sub, slot=sv, bank=bank: e.matmul(
                    pbf(bank)[:, 128 * sub:128 * (sub + 1)],
                    xf[:, 2048 * c + 128 * j: 2048 * c + 128 * (j + 1)],
                    wring[slot].rearrange("q (c o) -> q c o", o=128)[:, c, :],
                    start=(c == 0), stop=False) for c in range(8)]
                fns.append(lambda e, sub=sub, slot=sv, bank=bank: e.matmul(
                    pbf(bank)[:, 128 * sub:128 * (sub + 1)],
                    ones_bf[:, 0:128], bring[slot], start=False, stop=True))
            EG("tensor", fns,
               reads=["wr%d" % sv, "xf", "br%d" % sv, "ones_s"], writes=["pb%d" % bank])
            E("vector", lambda e, jj=jj, bank=bank, pair=pair: e.tensor_copy(
                Vaug8[pair].rearrange("q (j h c) -> q j h c", h=2, c=65)[:, 2 * jj:2 * jj + 2, :, 0:64],
                pbf(bank)[:, 0:256].rearrange("q (a h c) -> q a h c", a=2, h=2)),
                reads=["pb%d" % bank], writes=["Va8_%d" % pair])
        for jj in range(NCH // 2):
            bank = 4 + (jj % 4)
            fns = []
            for sub in range(2):
                j = 2 * jj + sub
                fns += [lambda e, j=j, sub=sub, half=half, bank=bank, pair=pair: e.matmul(
                    pbf(bank)[64 * half:64 * (half + 1), 136 * sub:136 * sub + 65],
                    Ktok[:, 128 * j + 64 * half: 128 * j + 64 * (half + 1)],
                    Vaug8[pair][:, 130 * j + 65 * half: 130 * j + 65 * (half + 1)],
                    start=True, stop=True) for half in range(2)]
            EG("tensor", fns, reads=["Ktok", "Va8_%d" % pair], writes=["pb%d" % bank])
            E("vector", lambda e, jj=jj, bank=bank: e.tensor_copy(
                scanA[:, 65 + 130 * jj:65 + 130 * (jj + 1)].rearrange("q (a c) -> q a c", a=2),
                pbf(bank)[:, 0:272].rearrange("q (a c) -> q a c", c=136)[:, :, 0:65]),
                reads=["pb%d" % bank], writes=["scanA"])
        EG("tensor", [lambda e, j=j, half=half, pair=pair: e.matmul(
            pbf(0)[64 * half:64 * (half + 1), 0:65],
            Ktok[:, 128 * j + 64 * half: 128 * j + 64 * (half + 1)],
            Vaug8[pair][:, 130 * j + 65 * half: 130 * j + 65 * (half + 1)],
            start=(j == 0), stop=(j == NCH - 1))
            for half in range(2) for j in range(NCH)],
            reads=["Ktok", "Va8_%d" % pair], writes=["pb0"])
        E("vector", lambda e: e.tensor_copy(Sfp, pbf(0)[:, 0:65]),
          reads=["pb0"], writes=["Sfp"])
        a3 = lambda ap, nb: ap.rearrange("q (j c) -> q j c", c=65)
        for kk in range(4):
            step = 1 << kk
            E("vector", lambda e, step=step: e.tensor_copy(
                scanB[:, 0:65 * step], scanA[:, 65:65 * (step + 1)]),
                reads=["scanA"], writes=["scanB"])
            E("vector", lambda e, step=step: e.tensor_tensor(
                scanB[:, 65 * step:1040], scanA[:, 65 * (step + 1):1105],
                scanA[:, 65:65 * (17 - step)], ALU.add),
                reads=["scanA"], writes=["scanB"])
            E("vector", lambda e: e.tensor_copy(scanA[:, 65:1105], scanB),
              reads=["scanB"], writes=["scanA"])
        DMA("sync", lambda e, pair=pair: e.dma_start(out=sloc_d[pair], in_=scanA[:, 0:1040]),
            reads=["scanA"], writes=["sloc_hbm"])
        for half in range(2):
            DMA("sync", lambda e, pair=pair, half=half: e.dma_start(
                out=cc_in.ap()[:, 65 * (2 * pair + half): 65 * (2 * pair + half + 1)],
                in_=view(b_Sfp, F32, 65, part=64, poff=64 * half)),
                reads=["Sfp"], writes=["cc_in"])

    if debug:
        DMA("sync", lambda e: e.dma_start(out=ccin_dbg[:], in_=cc_in.ap()[:]), reads=["cc_in"])
    if upto <= 2:
        return _finalize()

    qslot = {}

    def phaseC_load(pair):
        qslot[pair] = load_wslice(wqT_d, bq_d, pair)

    def phaseC_part1(pair):
        p.scope = "C1_%d" % pair
        if pair not in qslot:
            phaseC_load(pair)
        sq_ = qslot[pair]
        qkv_fmajor(sq_, Qf, "Qf", softplus=True)
        E("vector", lambda e: e.tensor_copy(Qm[64:128, :], Qf[64:128, :]),
          reads=["Qf"], writes=["Qm"])

    def phaseC_part2(pair):
        p.scope = "C2_%d" % pair
        Kfp = Kf8[pair]
        Vap = Vaug8[pair]
        rKf = "Kf8_%d" % pair
        rVa = "Va8_%d" % pair
        slr = "SlR%d" % (pair % 2)
        if pair + 1 < NPAIR:
            DMA("sync", lambda e, pair=pair: e.dma_start(
                out=SlocR[(pair + 1) % 2], in_=sloc_d[pair + 1]),
                reads=["sloc_hbm"], writes=["SlR%d" % ((pair + 1) % 2)])
        E("vector", lambda e, pair=pair, slr=slr: e.tensor_tensor(
            Saug.rearrange("q (j c) -> q j c", c=65),
            SlocR[pair % 2].rearrange("q (j c) -> q j c", c=65),
            Sprepb[:, 65 * pair:65 * (pair + 1)].rearrange("q (o c) -> q o c", o=1).broadcast_to((128, 16, 65)),
            ALU.add),
            reads=[slr, "Sprepb"], writes=["Sg%d" % _j for _j in range(16)])

        def stage1(uu):
            ab = uu % 2
            fns = []
            for sub in range(2):
                j = 2 * uu + sub
                fns += [
                    lambda e, j=j, sub=sub, ab=ab: e.matmul(
                        pbf(ab)[:, 256 * sub:256 * sub + 128],
                        Kfp[0:64, 128 * j:128 * (j + 1)],
                        Qf[0:64, 128 * j:128 * (j + 1)],
                        start=True, stop=True),
                    lambda e, j=j, sub=sub, ab=ab: e.matmul(
                        pbf(ab)[:, 256 * sub + 128:256 * sub + 256],
                        Kfp[:, 128 * j:128 * (j + 1)],
                        Qm[:, 128 * j:128 * (j + 1)],
                        start=True, stop=True)]
            EG("tensor", fns, reads=[rKf, "Qf", "Qm"], writes=["pb%d" % ab])
            E("vector", lambda e, uu=uu, ab=ab: e.tensor_tensor(
                Mp2[uu % 2], pbf(ab)[:, 0:512], mask2, ALU.mult),
                reads=["pb%d" % ab, "mask2_s"], writes=["Mp%d" % (uu % 2)])

        def stage2(uu):
            nb = 2 + (uu % 2)
            fns = []
            for sub in range(2):
                j = 2 * uu + sub
                co0 = 136 * sub
                fns += [
                    lambda e, j=j, sub=sub, nb=nb, co0=co0: e.matmul(
                        pbf(nb)[:, co0:co0 + 65], Mp2[(j // 2) % 2][:, 256 * sub:256 * sub + 128],
                        Vap[:, 130 * j:130 * j + 65],
                        start=True, stop=False),
                    lambda e, j=j, nb=nb, co0=co0: e.matmul(
                        pbf(nb)[:, co0:co0 + 65], Qf[0:64, 128 * j:128 * (j + 1)],
                        Saug[0:64, 65 * j:65 * (j + 1)],
                        start=False, stop=True),
                    lambda e, j=j, sub=sub, nb=nb, co0=co0: e.matmul(
                        pbf(nb)[:, co0 + 68:co0 + 133], Mp2[(j // 2) % 2][:, 256 * sub + 128:256 * sub + 256],
                        Vap[:, 130 * j + 65:130 * j + 130],
                        start=True, stop=False),
                    lambda e, j=j, nb=nb, co0=co0: e.matmul(
                        pbf(nb)[:, co0 + 68:co0 + 133], Qm[:, 128 * j:128 * (j + 1)],
                        Saug[:, 65 * j:65 * (j + 1)],
                        start=False, stop=True)]
            EG("tensor", fns,
               reads=["Mp%d" % (uu % 2), rVa, "Qf", "Qm",
                      "Sg%d" % (2 * uu), "Sg%d" % (2 * uu + 1)],
               writes=["pb%d" % nb])
            E("vector", lambda e, uu=uu, nb=nb: e.reciprocal(
                rden2[uu % 2],
                pbf(nb)[:, 0:272].rearrange("q (h c) -> q h c", c=68)[:, :, 64]),
                reads=["pb%d" % nb], writes=["rden%d" % (uu % 2)])
            for sub in range(2):
                for half in range(2):
                    co = 136 * sub + 68 * half
                    E("vector", lambda e, uu=uu, sub=sub, half=half, nb=nb, co=co: e.tensor_scalar(
                        otb2[uu % 2][:, 128 * sub + 64 * half:128 * sub + 64 * (half + 1)],
                        pbf(nb)[:, co:co + 64],
                        rden2[uu % 2][:, 2 * sub + half:2 * sub + half + 1], None, ALU.mult),
                        reads=["pb%d" % nb, "rden%d" % (uu % 2)], writes=["otb%d" % (uu % 2)])

        def stage3(uu):
            tb = 4 + (uu % 2)
            EG("tensor", [
                lambda e, uu=uu, tb=tb: e.transpose(
                    pbb(tb)[:, 0:128], otb2[uu % 2][:, 0:128], idb),
                lambda e, uu=uu, tb=tb: e.transpose(
                    pbb(tb)[:, 128:256], otb2[uu % 2][:, 128:256], idb)],
                reads=["otb%d" % (uu % 2), "idb_s"], writes=["pb%d" % tb])
            E("vector", lambda e, uu=uu, pair=pair, tb=tb: e.tensor_copy(
                af[:, 2048 * pair + 256 * uu: 2048 * pair + 256 * (uu + 1)], pbb(tb)[:, 0:256]),
                reads=["pb%d" % tb], writes=["af"])

        NU = NCH // 2
        for i in range(NU + 2):
            if i < NU:
                stage1(i)
            if 0 <= i - 1 < NU:
                stage2(i - 1)
            if 0 <= i - 2 < NU:
                stage3(i - 2)

    if upto >= 4:
        phaseC_part1(0)
        DMA("sync", lambda e: e.dma_start(out=SlocR[0], in_=sloc_d[0]),
            reads=["sloc_hbm"], writes=["SlR0"])

    p.scope = "coll"
    E("gpsimd", lambda e: e.collective_compute(
        "AllGather", ALU.bypass, replica_groups=[list(range(NCORE))],
        ins=[cc_in.ap()], outs=[cc_out.ap()]),
        reads=["cc_in"], writes=["cc_out"], sem="cc", amount=1)
    E("vector", lambda e: e.memset(Sprep, 0.0), writes=["Sprep"])
    for r in range(NCORE):
        for t in range(2):
            DMA("sync", lambda e, r=r, t=t: e.dma_start(
                out=gath[64 * t:64 * (t + 1), :].rearrange("q (pp c) -> q pp c", c=65),
                in_=cc_out.ap()[64 * r:64 * (r + 1), :].rearrange(
                    "q (pp two c) -> q two pp c", two=2, c=65)[:, t]),
                reads=["cc_out"], writes=["gath"])
        E("vector", lambda e, r=r: e.scalar_tensor_tensor(
            Sprep, gath, pm_s[:, r:r + 1], Sprep, ALU.mult, ALU.add),
            reads=["gath", "pm_s", "Sprep"], writes=["Sprep"])
    E("scalar", lambda e: e.copy(Sprepb, Sprep), reads=["Sprep"], writes=["Sprepb"])

    if debug:
        DMA("sync", lambda e: e.dma_start(out=sprep_dbg[:], in_=Sprep), reads=["Sprep"])
    if upto <= 3:
        return _finalize()

    phaseC_load(1)
    phaseC_part2(0)
    for pair in range(1, NPAIR):
        phaseC_part1(pair)
        if pair + 1 < NPAIR:
            phaseC_load(pair + 1)
        phaseC_part2(pair)
        if pair == 3:
            p.scope = "C2_%d" % pair
            DMA("gpsimd", lambda e: e.dma_start(out=bo_s, in_=bo_d[:]), writes=["bo_s"])
            DMA("gpsimd", lambda e: e.dma_start(out=wo_s, in_=woT_d), writes=["wo"])

    if debug:
        for n in range(16):
            DMA("sync", lambda e, n=n: e.dma_start(out=af_dbg[:, 1024 * n:1024 * (n + 1)],
                                                   in_=af[:, 1024 * n:1024 * (n + 1)]), reads=["af"])
    if upto <= 4:
        return _finalize()

    p.scope = "D"
    for n in range(NT):
        par = n % 2
        for oh in range(2):
            bank = (n * 2 + oh) % 4
            fns = [lambda e, n=n, oh=oh, c=c, bank=bank: e.matmul(
                pbf(bank), af[:, 2048 * c + 128 * n:2048 * c + 128 * (n + 1)],
                wo_s.rearrange("q (c o) -> q c o", o=1024)[:, c, 512 * oh:512 * (oh + 1)],
                start=(c == 0), stop=False) for c in range(8)]
            fns.append(lambda e, oh=oh, bank=bank: e.matmul(
                pbf(bank), ones_bf[:, 0:128], bo_s[:, 512 * oh:512 * (oh + 1)],
                start=False, stop=True))
            EG("tensor", fns, reads=["af", "wo", "ones_s", "bo_s"], writes=["pb%d" % bank])
        for oh in range(2):
            bank = (n * 2 + oh) % 4
            E("vector", lambda e, n=n, oh=oh, bank=bank, par=par: e.scalar_tensor_tensor(
                xt32[par][:, 512 * oh:512 * (oh + 1)], pbf(bank),
                eps_s, xbf[:, 1024 * n + 512 * oh:1024 * n + 512 * (oh + 1)],
                ALU.bypass, ALU.add, accum_out=mst[:, 4 * n + oh:4 * n + oh + 1]),
                reads=["pb%d" % bank, "xbf", "eps_s"], writes=["xt%d" % par, "mst"])
        E("vector", lambda e, n=n: e.tensor_tensor(
            mst[:, 4 * n:4 * n + 1], mst[:, 4 * n:4 * n + 1], mst[:, 4 * n + 1:4 * n + 2], ALU.add),
            reads=["mst"], writes=["mst"])
        E("scalar", lambda e, n=n: e.mul(mst[:, 4 * n + 1:4 * n + 2], mst[:, 4 * n:4 * n + 1], -1.0 / D),
          reads=["mst"], writes=["mst"])
        E("scalar", lambda e, n=n, par=par: e.activation(
            xsq2[par], xt32[par], AF.Square,
            bias=mst[:, 4 * n + 1:4 * n + 2], accum_out=mst[:, 4 * n + 2:4 * n + 3]),
            reads=["xt%d" % par, "mst"], writes=["xsq%d" % par, "mst"])
        E("scalar", lambda e, n=n: e.activation(
            mst[:, 4 * n + 2:4 * n + 3], mst[:, 4 * n + 2:4 * n + 3], AF.Ln,
            scale=1.0 / D, bias=eps_s),
            reads=["mst", "eps_s"], writes=["mst"])
        E("scalar", lambda e, n=n: e.activation(
            mst[:, 4 * n + 3:4 * n + 4], mst[:, 4 * n + 2:4 * n + 3], AF.Exp, scale=-0.5),
            reads=["mst"], writes=["mst"])
        E("vector", lambda e, n=n, par=par: e.tensor_scalar(
            xbf[:, 1024 * n:1024 * (n + 1)], xt32[par],
            mst[:, 4 * n + 1:4 * n + 2], mst[:, 4 * n + 3:4 * n + 4], ALU.add, ALU.mult),
            reads=["xt%d" % par, "mst"], writes=["xbf"])
        tile_transposes(n, "xbf")

    if debug:
        for n in range(16):
            DMA("sync", lambda e, n=n: e.dma_start(out=x1_dbg[:, 1024 * n:1024 * (n + 1)],
                                                   in_=xbf[:, 1024 * n:1024 * (n + 1)]), reads=["xbf"])
    if upto <= 5:
        return _finalize()

    p.scope = "R"
    for n in range(NT):
        bank = 6 + (n % 2)
        EG("tensor", [lambda e, n=n, c=c, bank=bank: e.matmul(
            pbf(bank)[:, 0:16], xf[:, 2048 * c + 128 * n:2048 * c + 128 * (n + 1)],
            gw_s.rearrange("q (c e) -> q c e", e=16)[:, c, :],
            start=(c == 0), stop=(c == 7)) for c in range(8)],
            reads=["xf", "gw"], writes=["pb%d" % bank])
        E("scalar", lambda e, n=n, bank=bank: e.activation(scores[:, 16 * n:16 * (n + 1)], pbf(bank)[:, 0:16], AF.Sigmoid),
          reads=["pb%d" % bank], writes=["scores"])
    E("vector", lambda e: e.tensor_tensor(
        tmpr.rearrange("q (t e) -> q t e", e=16), scores.rearrange("q (t e) -> q t e", e=16),
        ebias_s.rearrange("q (o e) -> q o e", o=1).broadcast_to((128, 16, 16)),
        ALU.add), reads=["scores", "ebias_s"], writes=["tmpr"])
    pairs = [(0, 1), (0, 2), (0, 3), (1, 2), (1, 3), (2, 3)]
    for idx, (i, jj) in enumerate(pairs):
        if idx == 0:
            E("vector", lambda e, i=i, jj=jj: e.tensor_tensor(
                gall, tmpr[:, i::4], tmpr[:, jj::4], ALU.add), reads=["tmpr"], writes=["gall"])
        else:
            E("vector", lambda e, i=i, jj=jj: e.tensor_tensor(
                eqb[:, 0:64], tmpr[:, i::4], tmpr[:, jj::4], ALU.add), reads=["tmpr"], writes=["eqb"])
            E("vector", lambda e: e.tensor_tensor(gall, gall, eqb[:, 0:64], ALU.max),
              reads=["gall", "eqb"], writes=["gall"])
    for idx, (i, jj) in enumerate(pairs):
        if idx == 0:
            E("vector", lambda e, i=i, jj=jj: e.tensor_tensor(
                tau, gall[:, i::4], gall[:, jj::4], ALU.min), reads=["gall"], writes=["tau"])
        else:
            E("vector", lambda e, i=i, jj=jj: e.tensor_tensor(
                rsc[:, 0:16], gall[:, i::4], gall[:, jj::4], ALU.min), reads=["gall"], writes=["rsc"])
            E("vector", lambda e: e.tensor_tensor(tau, tau, rsc[:, 0:16], ALU.max),
              reads=["tau", "rsc"], writes=["tau"])
    for g in range(4):
        E("vector", lambda e, g=g: e.tensor_tensor(gm[:, g::4], gall[:, g::4], tau, ALU.is_ge),
          reads=["gall", "tau"], writes=["gm"])
    for i in range(4):
        E("vector", lambda e, i=i: e.tensor_tensor(tmpr[:, i::4], tmpr[:, i::4], gm, ALU.mult),
          reads=["tmpr", "gm"], writes=["tmpr"])
    E("vector", lambda e: e.memset(comb, 0.0), writes=["comb"])
    t3 = lambda ap: ap.rearrange("q (t e) -> q t e", e=16)
    for rnd in range(4):
        E("vector", lambda e: e.reduce_max(rsc[:, 16:32].rearrange("q (t e) -> q t e", e=1), t3(tmpr), axis=AX.X),
          reads=["tmpr"], writes=["rsc"])
        E("vector", lambda e: e.tensor_tensor(
            t3(eqb), t3(tmpr), rsc[:, 16:32].rearrange("q (t e) -> q t e", e=1).broadcast_to((128, 16, 16)), ALU.is_ge),
            reads=["tmpr", "rsc"], writes=["eqb"])
        E("vector", lambda e: e.tensor_tensor(eqb, eqb, iotmb, ALU.mult),
          reads=["eqb", "iotmb_s"], writes=["eqb"])
        E("vector", lambda e: e.tensor_reduce(
            rsc[:, 16:32].rearrange("q (t e) -> q t e", e=1), t3(eqb), AX.X, ALU.min),
            reads=["eqb"], writes=["rsc"])
        E("vector", lambda e: e.tensor_tensor(
            t3(eqb), t3(iotmb), rsc[:, 16:32].rearrange("q (t e) -> q t e", e=1).broadcast_to((128, 16, 16)), ALU.is_equal),
            reads=["iotmb_s", "rsc"], writes=["eqb"])
        E("vector", lambda e: e.tensor_tensor(comb, comb, eqb, ALU.add),
          reads=["comb", "eqb"], writes=["comb"])
        if rnd < 3:
            E("vector", lambda e: e.scalar_tensor_tensor(
                tmpr, eqb, nbig_s, tmpr, ALU.mult, ALU.add),
                reads=["eqb", "nbig_s", "tmpr"], writes=["tmpr"])
    E("vector", lambda e: e.tensor_tensor(comb, comb, scores, ALU.mult),
      reads=["comb", "scores"], writes=["comb"])
    E("vector", lambda e: e.reduce_sum(
        rsc[:, 16:32].rearrange("q (t e) -> q t e", e=1), t3(comb), axis=AX.X),
        reads=["comb"], writes=["rsc"])
    E("vector", lambda e: e.tensor_scalar(rsc[:, 16:32], rsc[:, 16:32], tiny_s, None, ALU.add),
      reads=["rsc", "tiny_s"], writes=["rsc"])
    E("vector", lambda e: e.reciprocal(rsc[:, 0:16], rsc[:, 16:32]),
      reads=["rsc"], writes=["rsc"])
    E("vector", lambda e: e.tensor_tensor(
        t3(comb), t3(comb), rsc[:, 0:16].rearrange("q (t e) -> q t e", e=1).broadcast_to((128, 16, 16)), ALU.mult),
        reads=["comb", "rsc"], writes=["comb"])

    if debug:
        DMA("sync", lambda e: e.dma_start(out=comb_dbg[:], in_=comb), reads=["comb"])
    if upto <= 6:
        return _finalize()

    eun = [0]

    def load_eunit(which, e_idx):
        u = eun[0]; eun[0] += 1
        slot = u % 8
        srcs = (gateT_d, upT_d, downT_d)[which]
        DMA("gpsimd", lambda e, srcs=srcs, e_idx=e_idx, slot=slot: e.dma_start(
            out=ers[slot], in_=srcs[e_idx]),
            writes=["er%d" % slot])
        return slot

    for hf in range(2):
        p.scope = "FFN%d" % hf
        for ei in range(NEXP):
            sgate = load_eunit(0, ei)
            sup = load_eunit(1, ei)
            sdown = load_eunit(2, ei)
            for ic in range(4):
                for u in range(2):
                    it = ic * 2 + u
                    gbank = it % 2
                    ubank = 2 + it % 2
                    gsp = gs2[it % 2]
                    EG("tensor", [lambda e, ic=ic, u=u, c=c, gbank=gbank, sgate=sgate, hf=hf: e.matmul(
                        pbf(gbank), ers[sgate].rearrange("q (c i) -> q c i", c=8)[:, c, 128 * ic:128 * (ic + 1)],
                        xf[:, 2048 * c + 1024 * hf + 512 * u: 2048 * c + 1024 * hf + 512 * (u + 1)],
                        start=(c == 0), stop=(c == 7)) for c in range(8)],
                        reads=["er%d" % sgate, "xf"], writes=["pb%d" % gbank])
                    EG("tensor", [lambda e, ic=ic, u=u, c=c, ubank=ubank, sup=sup, hf=hf: e.matmul(
                        pbf(ubank), ers[sup].rearrange("q (c i) -> q c i", c=8)[:, c, 128 * ic:128 * (ic + 1)],
                        xf[:, 2048 * c + 1024 * hf + 512 * u: 2048 * c + 1024 * hf + 512 * (u + 1)],
                        start=(c == 0), stop=(c == 7)) for c in range(8)],
                        reads=["er%d" % sup, "xf"], writes=["pb%d" % ubank])
                    E("scalar", lambda e, gbank=gbank, gsp=gsp: e.activation(gsp, pbf(gbank), AF.Silu),
                      reads=["pb%d" % gbank], writes=["gs%d" % (it % 2)])
                    E("vector", lambda e, ic=ic, u=u, ubank=ubank, gsp=gsp: e.tensor_tensor(
                        actb[:, 1024 * ic + 512 * u: 1024 * ic + 512 * (u + 1)], gsp, pbf(ubank), ALU.mult),
                        reads=["gs%d" % (it % 2), "pb%d" % ubank], writes=["act%d_%d" % (ic, u)])
            for j in range(8):
                nglob = 8 * hf + j
                for oh in range(2):
                    bank = 4 + (j * 2 + oh) % 4
                    EG("tensor", [lambda e, j=j, oh=oh, ic=ic, bank=bank, sdown=sdown: e.matmul(
                        pbf(bank), actb[:, 1024 * ic + 128 * j: 1024 * ic + 128 * (j + 1)],
                        ers[sdown].rearrange("q (c o) -> q c o", c=4)[:, ic, 512 * oh:512 * (oh + 1)],
                        start=(ic == 0), stop=(ic == 3)) for ic in range(4)],
                        reads=["act%d_%d" % (ic2, j // 4) for ic2 in range(4)] + ["er%d" % sdown],
                        writes=["pb%d" % bank])
                    if ei == 0:
                        E("vector", lambda e, j=j, oh=oh, bank=bank, nglob=nglob: e.tensor_scalar(
                            yv[:, 1024 * j + 512 * oh:1024 * j + 512 * (oh + 1)], pbf(bank),
                            comb[:, 16 * nglob + 0:16 * nglob + 1], None, ALU.mult),
                            reads=["pb%d" % bank, "comb"], writes=["yv"])
                    else:
                        E("vector", lambda e, j=j, oh=oh, bank=bank, nglob=nglob, ei=ei: e.scalar_tensor_tensor(
                            yv[:, 1024 * j + 512 * oh:1024 * j + 512 * (oh + 1)], pbf(bank),
                            comb[:, 16 * nglob + ei:16 * nglob + ei + 1],
                            yv[:, 1024 * j + 512 * oh:1024 * j + 512 * (oh + 1)], ALU.mult, ALU.add),
                            reads=["pb%d" % bank, "comb", "yv"], writes=["yv"])
        p.scope = "LN2_%d" % hf
        for j in range(8):
            n = 8 * hf + j
            mb = 64 + 4 * n
            par = n % 2
            E("vector", lambda e, j=j, n=n, mb=mb, par=par: e.scalar_tensor_tensor(
                xsq2[par], yv[:, 1024 * j:1024 * (j + 1)], eps_s,
                xbf[:, 1024 * n:1024 * (n + 1)], ALU.bypass, ALU.add,
                accum_out=mst[:, mb:mb + 1]),
                reads=["yv", "xbf", "eps_s"], writes=["xsq%d" % par, "mst"])
            E("scalar", lambda e, mb=mb: e.mul(mst[:, mb + 1:mb + 2], mst[:, mb:mb + 1], -1.0 / D),
              reads=["mst"], writes=["mst"])
            E("scalar", lambda e, mb=mb, par=par: e.activation(
                xsqB[:, 0:1024], xsq2[par], AF.Square, bias=mst[:, mb + 1:mb + 2], accum_out=mst[:, mb + 2:mb + 3]),
                reads=["xsq%d" % par, "mst"], writes=["xsqB", "mst"])
            E("scalar", lambda e, mb=mb: e.activation(
                mst[:, mb + 2:mb + 3], mst[:, mb + 2:mb + 3], AF.Ln, scale=1.0 / D, bias=eps_s),
                reads=["mst", "eps_s"], writes=["mst"])
            E("scalar", lambda e, mb=mb: e.activation(mst[:, mb + 3:mb + 4], mst[:, mb + 2:mb + 3], AF.Exp, scale=-0.5),
              reads=["mst"], writes=["mst"])
            E("vector", lambda e, mb=mb, par=par: e.tensor_scalar(
                xsqB[:, 0:1024], xsq2[par], mst[:, mb + 1:mb + 2], mst[:, mb + 3:mb + 4], ALU.add, ALU.mult),
                reads=["xsq%d" % par, "mst"], writes=["xsqB"])
            DMA("sync", lambda e, n=n: e.dma_start(
                out=out_d[128 * n:128 * (n + 1), :], in_=xsqB[:, 0:1024]),
                reads=["xsqB"], writes=["outd"])

    return _finalize()


_CACHE = {}


def make_in_maps(inputs):
    x = np.asarray(inputs["L0_attn_x"], dtype=np.float32).reshape(S, D)
    wq = np.asarray(inputs["L0_attn_Wq"], dtype=np.float32)
    wk = np.asarray(inputs["L0_attn_Wk"], dtype=np.float32)
    wv = np.asarray(inputs["L0_attn_Wv"], dtype=np.float32)
    wo = np.asarray(inputs["L0_attn_Wo"], dtype=np.float32)
    bq = np.asarray(inputs["L0_attn_bq"], dtype=np.float32).reshape(1, D)
    bk = np.asarray(inputs["L0_attn_bk"], dtype=np.float32).reshape(1, D)
    bv = np.asarray(inputs["L0_attn_bv"], dtype=np.float32).reshape(1, D)
    bo = np.asarray(inputs["L0_attn_bo"], dtype=np.float32).reshape(1, D)
    gw = np.asarray(inputs["L0_ffn_gate_weight"], dtype=np.float32)
    ebias = np.asarray(inputs["L0_ffn_e_bias"], dtype=np.float32)
    gate = np.asarray(inputs["L0_ffn_gate_proj"], dtype=np.float32)
    up = np.asarray(inputs["L0_ffn_up_proj"], dtype=np.float32)
    down = np.asarray(inputs["L0_ffn_down_proj"], dtype=np.float32)

    def _sw_qkv(w):
        return np.ascontiguousarray(w.T.reshape(8, 128, 8, 128).transpose(2, 1, 0, 3).reshape(8, 128, 1024))
    wqT = _sw_qkv(wq)
    wkT = _sw_qkv(wk)
    wvT = _sw_qkv(wv)
    woT = np.ascontiguousarray(wo.T.reshape(8, 128, 1024).transpose(1, 0, 2).reshape(128, 8192))
    gwT = np.ascontiguousarray(gw.T.reshape(8, 128, 16).transpose(1, 0, 2).reshape(128, 128))
    gateT = np.ascontiguousarray(gate.transpose(0, 2, 1).reshape(16, 8, 128, 512).transpose(0, 2, 1, 3).reshape(16, 128, 4096))
    upT = np.ascontiguousarray(up.transpose(0, 2, 1).reshape(16, 8, 128, 512).transpose(0, 2, 1, 3).reshape(16, 128, 4096))
    downT = np.ascontiguousarray(down.transpose(0, 2, 1).reshape(16, 4, 128, 1024).transpose(0, 2, 1, 3).reshape(16, 128, 4096))
    ebias_t = np.ascontiguousarray(np.broadcast_to(ebias.reshape(1, NEXP), (128, NEXP)))
    idb = np.eye(128, dtype=ml_dtypes.bfloat16)
    m = np.triu(np.ones((128, 128), np.float32))
    mask2 = np.concatenate([m, m, m, m], axis=1)
    iotmb = np.ascontiguousarray(np.broadcast_to(np.tile(np.arange(16, dtype=np.float32), 16)[None, :], (128, 256)) - BIGV)

    in_maps = []
    for c in range(NCORE):
        pmask = np.zeros((128, NCORE), np.float32)
        pmask[:, :c] = 1.0
        in_maps.append({
            "x": np.ascontiguousarray(x[TC * c:TC * (c + 1)]),
            "wqT": wqT, "wkT": wkT, "wvT": wvT, "woT": woT,
            "bq": bq, "bk": bk, "bv": bv, "bo": bo,
            "gwT": gwT, "ebias": ebias_t,
            "gateT": gateT, "upT": upT, "downT": downT,
            "idb": idb, "mask2": mask2,
            "iotmb": iotmb, "pmask": pmask,
        })
    return in_maps


def run_probe(inputs, upto=99, debug=True, trace=False):
    nc, es, prog = build_program(upto=upto, debug=debug)
    print("probe upto=%d: n_ops=%d n_waits=%d" % (upto, prog.n_ops, prog.n_waits))
    in_maps = make_in_maps(inputs)
    res = run_bass_kernel_spmd(nc, in_maps, core_ids=list(range(NCORE)), trace=trace)
    return res


def kernel(**inputs):
    if "nc" not in _CACHE:
        nc, es, prog = build_program()
        _CACHE["nc"] = nc
        _CACHE["es"] = es
    nc = _CACHE["nc"]
    in_maps = make_in_maps(inputs)
    res = run_bass_kernel_spmd(nc, in_maps, core_ids=list(range(NCORE)))
    out = np.concatenate([np.asarray(r["out"], dtype=np.float32) for r in res.results], axis=0)
    return out.reshape(1, S, D)


# version 13
# speedup vs baseline: 1.2368x; 1.2368x over previous
import numpy as np
import ml_dtypes
import concourse.bass as bass
import concourse.mybir as mybir
from concourse.bass_utils import run_bass_kernel_spmd
from contextlib import ExitStack

F32 = mybir.dt.float32
BF16 = mybir.dt.bfloat16
U8 = mybir.dt.uint8
AF = mybir.ActivationFunctionType
ALU = mybir.AluOpType
AX = mybir.AxisListType

NCORE = 8
S = 16384
TC = 2048
D = 1024
NH = 16
DH = 64
NPAIR = 8
NCH = 16
NT = 16
NEXP = 16
DI = 512
BIGV = 1.0e4

ENGINES = ("tensor", "vector", "scalar", "sync", "gpsimd")
CSEM = {"tensor": "st", "vector": "sv", "scalar": "ss", "sync": "sp", "gpsimd": "sg"}
QPOOL = {"sync": 3, "gpsimd": 8}


class Prog:

    def __init__(self):
        self.stream = {e: [] for e in ENGINES}
        self.cnt = {}
        self.reg = {}
        self.waited = {e: {} for e in ENGINES}
        self.iv = {}
        self.overlaps = {}
        self.n_ops = 0
        self.n_waits = 0
        self.scope = "init"

    def interval(self, name, space, lo, hi):
        self.iv[name] = (space, lo, hi)
        self.overlaps = {}

    def _family(self, name):
        got = self.overlaps.get(name)
        if got is not None:
            return got
        if name not in self.iv:
            fam = (name,)
        else:
            sp, lo, hi = self.iv[name]
            fam = tuple(
                o for o, (sp2, lo2, hi2) in self.iv.items()
                if sp2 == sp and lo < hi2 and lo2 < hi
            )
        self.overlaps[name] = fam
        return fam

    def emit(self, eng, fn, reads=(), writes=(), sem=None, amount=1):
        if sem is None:
            sem = CSEM[eng]
        waits = {}

        def need(m):
            if m is None:
                return
            s, v = m
            if s == CSEM[eng] and eng == "tensor":
                return
            if self.waited[eng].get(s, 0) >= v:
                return
            waits[s] = max(waits.get(s, 0), v)

        rfam, wfam = set(), set()
        for r in reads:
            rfam.update(self._family(r))
        for r in writes:
            wfam.update(self._family(r))
        for r in rfam:
            e = self.reg.get(r)
            if e:
                need(e["w"])
        for r in wfam:
            e = self.reg.get(r)
            if e:
                need(e["w"])
                for s, v in e["r"].items():
                    need((s, v))
        for s, v in sorted(waits.items()):
            self.stream[eng].append(("wait", s, v))
            self.waited[eng][s] = max(self.waited[eng].get(s, 0), v)
            self.n_waits += 1
        self.cnt[sem] = self.cnt.get(sem, 0) + amount
        v_after = self.cnt[sem]
        self.stream[eng].append(("op", fn, sem, amount, self.scope))
        self.n_ops += 1
        for r in rfam:
            self.reg.setdefault(r, {"w": None, "r": {}})["r"][sem] = v_after
        for r in writes:
            for rr in self._family(r):
                if rr == r:
                    self.reg[r] = {"w": (sem, v_after), "r": {}}
                else:
                    self.reg[rr] = {"w": (sem, v_after), "r": {}}

    def dma(self, eng, fn, reads=(), writes=()):
        pool = QPOOL[eng]
        i = self.cnt.get("dmaidx_" + eng, 0)
        self.cnt["dmaidx_" + eng] = i + 1
        sem = "q%s%d" % (eng[0], i % pool)
        gate = 16 * (i // pool)
        if gate > 0 and self.waited[eng].get(sem, 0) < gate:
            self.stream[eng].append(("wait", sem, gate))
            self.waited[eng][sem] = gate
            self.n_waits += 1
        self.emit(eng, fn, reads, writes, sem=sem, amount=16)


def build_program(upto=99, debug=False, c_pairs=NPAIR, c_step=9):
    nc = bass.Bass("TRN2", target_bir_lowering=False)

    x_d = nc.dram_tensor("x", [TC, D], F32, kind="ExternalInput").ap()
    wqT_d = nc.dram_tensor("wqT", [NPAIR, 128, D], F32, kind="ExternalInput").ap()
    wkT_d = nc.dram_tensor("wkT", [NPAIR, 128, D], F32, kind="ExternalInput").ap()
    wvT_d = nc.dram_tensor("wvT", [NPAIR, 128, D], F32, kind="ExternalInput").ap()
    woT_d = nc.dram_tensor("woT", [128, 8 * D], F32, kind="ExternalInput").ap()
    bq_d = nc.dram_tensor("bq", [1, D], F32, kind="ExternalInput").ap()
    bk_d = nc.dram_tensor("bk", [1, D], F32, kind="ExternalInput").ap()
    bv_d = nc.dram_tensor("bv", [1, D], F32, kind="ExternalInput").ap()
    bo_d = nc.dram_tensor("bo", [1, D], F32, kind="ExternalInput").ap()
    gwT_d = nc.dram_tensor("gwT", [128, 8 * NEXP], F32, kind="ExternalInput").ap()
    ebias_d = nc.dram_tensor("ebias", [128, NEXP], F32, kind="ExternalInput").ap()
    gateT_d = nc.dram_tensor("gateT", [NEXP, 128, 8 * DI], F32, kind="ExternalInput").ap()
    upT_d = nc.dram_tensor("upT", [NEXP, 128, 8 * DI], F32, kind="ExternalInput").ap()
    downT_d = nc.dram_tensor("downT", [NEXP, 128, 4 * D], F32, kind="ExternalInput").ap()
    idb_d = nc.dram_tensor("idb", [128, 128], BF16, kind="ExternalInput").ap()
    mask2_d = nc.dram_tensor("mask2", [128, 512], F32, kind="ExternalInput").ap()
    iotmb_d = nc.dram_tensor("iotmb", [128, 256], F32, kind="ExternalInput").ap()
    pmask_d = nc.dram_tensor("pmask", [128, NCORE], F32, kind="ExternalInput").ap()
    out_d = nc.dram_tensor("out", [TC, D], F32, kind="ExternalOutput").ap()

    cc_in = nc.dram_tensor("cc_in", [64, NH * 65], F32)
    sloc_d = nc.dram_tensor("sloc_d", [NPAIR, 128, NCH * 65], BF16)
    cc_out = nc.dram_tensor("cc_out", [64 * NCORE, NH * 65], F32)
    if debug:
        xf_dbg = nc.dram_tensor("xf_dbg", [128, 16384], BF16, kind="ExternalOutput").ap()
        ccin_dbg = nc.dram_tensor("ccin_dbg", [64, NH * 65], F32, kind="ExternalOutput").ap()
        sprep_dbg = nc.dram_tensor("sprep_dbg", [128, 520], F32, kind="ExternalOutput").ap()
        af_dbg = nc.dram_tensor("af_dbg", [128, 16384], BF16, kind="ExternalOutput").ap()
        x1_dbg = nc.dram_tensor("x1_dbg", [128, 16384], BF16, kind="ExternalOutput").ap()
        comb_dbg = nc.dram_tensor("comb_dbg", [128, 256], F32, kind="ExternalOutput").ap()

    p = Prog()
    es = ExitStack()

    SBUF_BYTES = 212736
    arena = es.enter_context(nc.sbuf_tensor("arena", [128, SBUF_BYTES], U8))
    alloc_off = [0]

    def carve(name, nbytes, align=64):
        off = (alloc_off[0] + align - 1) // align * align
        alloc_off[0] = off + nbytes
        assert alloc_off[0] <= SBUF_BYTES, f"SBUF overflow at {name}: {alloc_off[0]}"
        return name, off

    def view(buf, dtype, cols, part=128, poff=0):
        name, off = buf
        isz = 2 if dtype in (BF16,) else (4 if dtype in (F32,) else 1)
        apv = arena[:].bitcast(dtype)
        ap2 = apv[poff:poff + part, off // isz: off // isz + cols]
        return ap2

    def reg_iv(buf, nbytes, name=None):
        nm, off = buf
        p.interval(name or nm, "sbuf", off, off + nbytes)

    b_xf = carve("xf", 32768); reg_iv(b_xf, 32768)
    b_xbf = carve("xbf", 32768); reg_iv(b_xbf, 32768)
    b_gs = carve("gs", 2 * 2048)
    for _k in range(2):
        p.interval("gs%d" % _k, "sbuf", b_gs[1] + 2048 * _k, b_gs[1] + 2048 * (_k + 1))
    b_idb = carve("idb_s", 256); reg_iv(b_idb, 256)
    b_pm = carve("pm_s", 32); reg_iv(b_pm, 32)
    b_ones = carve("ones_s", 1024); reg_iv(b_ones, 1024)
    b_eps = carve("eps_s", 4); reg_iv(b_eps, 4)
    b_nbig = carve("nbig_s", 4); reg_iv(b_nbig, 4)
    b_tiny = carve("tiny_s", 4); reg_iv(b_tiny, 4)
    b_iotmb = carve("iotmb_s", 1024); reg_iv(b_iotmb, 1024)
    b_gw = carve("gw", 256); reg_iv(b_gw, 256)
    b_ebias = carve("ebias_s", 64); reg_iv(b_ebias, 64)

    AOFF = (alloc_off[0] + 63) // 64 * 64
    U = lambda off: AOFF + off

    def ubuf(name, off, nbytes):
        b = (name, U(off))
        p.interval(name, "sbuf", U(off), U(off) + nbytes)
        return b

    b_af = ubuf("af", 0, 32768)
    b_Kf8 = [ubuf("Kf8_%d" % _k, 32768 + 4096 * _k, 4096) for _k in range(8)]
    b_Vaug8 = [ubuf("Va8_%d" % _k, 65536 + 4160 * _k, 4160) for _k in range(8)]
    b_wring = (None, U(98816))
    for _k in range(4):
        p.interval("wr%d" % _k, "sbuf", U(98816) + 2048 * _k, U(98816) + 2048 * (_k + 1))
    b_bring = (None, U(107008))
    for _k in range(4):
        p.interval("br%d" % _k, "sbuf", U(107008) + 256 * _k, U(107008) + 256 * (_k + 1))
    b_Qf = ubuf("Qf", 108032, 4096)
    b_Qm = ubuf("Qm", 112128, 4096)
    b_scanA = ("scanA", U(108032))
    p.interval("scanA", "sbuf", U(108032), U(108032) + 2210)
    b_scanB = ("scanB", U(120320))
    p.interval("scanB", "sbuf", U(120320), U(120320) + 2080)
    b_Ktok = ubuf("Ktok", 116224, 4096)
    b_SlocS = ubuf("SlocS", 120320, 2080)
    b_SlocR = (None, U(122400))
    for _k in range(2):
        p.interval("SlR%d" % _k, "sbuf", U(122400) + 2080 * _k, U(122400) + 2080 * (_k + 1))
    b_Saug = (None, U(126560))
    for _k in range(16):
        p.interval("Sg%d" % _k, "sbuf", U(126560) + 130 * _k, U(126560) + 130 * (_k + 1))
    b_Sfp = ubuf("Sfp", 128640, 260)
    b_Mp = (None, U(128960))
    for _k in range(2):
        p.interval("Mp%d" % _k, "sbuf", U(128960) + 1024 * _k, U(128960) + 1024 * (_k + 1))
    b_otb = (None, U(131008))
    for _k in range(2):
        p.interval("otb%d" % _k, "sbuf", U(131008) + 512 * _k, U(131008) + 512 * (_k + 1))
    b_rden = (None, U(132032))
    for _k in range(2):
        p.interval("rden%d" % _k, "sbuf", U(132032) + 16 * _k, U(132032) + 16 * (_k + 1))
    b_Sprep = ubuf("Sprep", 132096, 2080)
    b_Sprepb = ubuf("Sprepb", 134176, 1040)
    b_gath = ubuf("gath", 135232, 2080)
    b_mask2 = ubuf("mask2_s", 137344, 2048)
    UNEED_ATTN = 139392

    b_er = [ubuf("er%d" % _k, 8192 * _k, 8192) for _k in range(6)]
    b_wo = ubuf("wo", 32768, 16384)
    b_act = (None, U(49152))
    for _ic in range(4):
        for _u in range(2):
            p.interval("act%d_%d" % (_ic, _u), "sbuf",
                       U(49152) + 2048 * _ic + 1024 * _u,
                       U(49152) + 2048 * _ic + 1024 * (_u + 1))
    b_yv = ubuf("yv", 57344, 32768)
    b_xt32 = (None, U(90112))
    for _k in range(2):
        p.interval("xt%d" % _k, "sbuf", U(90112) + 4096 * _k, U(90112) + 4096 * (_k + 1))
    b_xsq = (None, U(98304))
    for _k in range(2):
        p.interval("xsq%d" % _k, "sbuf", U(98304) + 4096 * _k, U(98304) + 4096 * (_k + 1))
    b_xsqB = ubuf("xsqB", 106496, 4096)
    b_scores = ubuf("scores", 110592, 1024)
    b_comb = ubuf("comb", 111616, 1024)
    b_tmpr = ubuf("tmpr", 112640, 1024)
    b_eq = ubuf("eqb", 113664, 1024)
    b_gall = ubuf("gall", 114688, 256)
    b_tau = ubuf("tau", 114944, 64)
    b_gm = ubuf("gm", 115008, 256)
    b_rsc = ubuf("rsc", 115264, 128)
    b_mst = ubuf("mst", 115392, 768)
    b_bo = ubuf("bo_s", 116160, 2048)
    UNEED = max(UNEED_ATTN, 118208)
    alloc_off[0] = AOFF + UNEED
    assert alloc_off[0] <= SBUF_BYTES, alloc_off[0]

    xf = view(b_xf, BF16, 16384)
    xbf = view(b_xbf, BF16, 16384)
    af = view(b_af, BF16, 16384)
    yv = view(b_yv, F32, 8192)
    wo_s = view(b_wo, BF16, 8192)
    Qf = view(b_Qf, BF16, 2048)
    Qm = view(b_Qm, BF16, 2048)
    scanA = view(b_scanA, BF16, 1105)
    scanB = view((None, U(120320)), BF16, 1040)
    Kf8 = [view(b_Kf8[_k], BF16, 2048) for _k in range(8)]
    Vaug8 = [view(b_Vaug8[_k], BF16, 2080) for _k in range(8)]
    Ktok = view(b_Ktok, BF16, 2048)
    SlocS = view(b_SlocS, BF16, 1040)
    SlocR = [view((None, U(122400) + 2080 * _k), BF16, 1040) for _k in range(2)]
    Mp2 = [view((None, U(128960) + 1024 * _k), BF16, 512) for _k in range(2)]
    otb2 = [view((None, U(131008) + 512 * _k), BF16, 256) for _k in range(2)]
    Saug = view((None, U(126560)), BF16, 1040)
    Sfp = view(b_Sfp, F32, 65)
    rden2 = [view((None, U(132032) + 16 * _k), F32, 4) for _k in range(2)]
    wring = [view((None, U(98816) + 2048 * _k), BF16, 1024) for _k in range(4)]
    bring = [view((None, U(107008) + 256 * _k), BF16, 128, part=1) for _k in range(4)]
    gs2 = [view((b_gs[0], b_gs[1] + 2048 * _k), F32, 512) for _k in range(2)]
    ers = [view(b_er[_k], BF16, 4096) for _k in range(6)]
    actb = view(b_act, BF16, 4096)
    Sprep = view(b_Sprep, F32, 520)
    Sprepb = view(b_Sprepb, BF16, 520)
    gath = view(b_gath, F32, 520)
    scores = view(b_scores, F32, 256)
    comb = view(b_comb, F32, 256)
    tmpr = view(b_tmpr, F32, 256)
    eqb = view(b_eq, F32, 256)
    gall = view(b_gall, F32, 64)
    tau = view(b_tau, F32, 16)
    gm = view(b_gm, F32, 64)
    rsc = view(b_rsc, F32, 32)
    mst = view(b_mst, F32, 192)
    xt32 = [view((None, U(90112) + 4096 * _k), F32, 1024) for _k in range(2)]
    xsq2 = [view((None, U(98304) + 4096 * _k), F32, 1024) for _k in range(2)]
    xsqB = view(b_xsqB, F32, 1024)
    gw_s = view(b_gw, BF16, 128)
    bo_s = view(b_bo, BF16, 1024, part=1)
    idb = view(b_idb, BF16, 128)
    mask2 = view(b_mask2, F32, 512)
    iotmb = view(b_iotmb, F32, 256)
    pm_s = view(b_pm, F32, 8)
    ones_bf = view(b_ones, BF16, 512, part=1)
    eps_s = view(b_eps, F32, 1)
    nbig_s = view(b_nbig, F32, 1)
    tiny_s = view(b_tiny, F32, 1)
    ebias_s = view(b_ebias, F32, 16)

    pb = [es.enter_context(nc.psum_tensor("pbank%d" % _k, [128, 512], F32)) for _k in range(8)]
    for k in range(8):
        p.interval("pb%d" % k, "psum", 2048 * k, 2048 * (k + 1))

    def pbf(k):
        return pb[k][:]

    def pbb(k):
        return pb[k][:].bitcast(BF16)

    E = p.emit
    DMA = p.dma

    def EG(eng, fns, reads=(), writes=()):
        def run(e, fns=tuple(fns)):
            inst = None
            for f in fns:
                inst = f(e)
            return inst
        E(eng, run, reads=reads, writes=writes)

    def _finalize():
        sem_names = sorted({s for e in ENGINES for item in p.stream[e]
                            for s in [item[1] if item[0] == "wait" else item[2]]})
        handles = {}
        for s in sem_names:
            handles[s] = es.enter_context(nc.semaphore(s))
        qfinal = {s: v for s, v in p.cnt.items() if s.startswith("q") and not s.startswith("dmaidx")}
        with nc.Block() as block:
            def run_stream(engine, name):
                cur = None
                cur_id = None
                for item in p.stream[name]:
                    if item[0] == "wait":
                        engine.wait_ge(handles[item[1]], item[2])
                    else:
                        _, fn, sem, amount, scope = item
                        if scope != cur:
                            if cur is not None:
                                nc.leave_named_scope(cur, cur_id, False)
                            cur_id, _ = nc.enter_named_scope(scope, False)
                            cur = scope
                        inst = fn(engine)
                        inst.then_inc(handles[sem], amount)
                if cur is not None:
                    nc.leave_named_scope(cur, cur_id, False)
                if name == "sync":
                    for s, v in sorted(qfinal.items()):
                        engine.wait_ge(handles[s], v)

            @block.sync
            def _(e):
                run_stream(e, "sync")

            @block.gpsimd
            def _(e):
                run_stream(e, "gpsimd")

            @block.tensor
            def _(e):
                run_stream(e, "tensor")

            @block.vector
            def _(e):
                run_stream(e, "vector")

            @block.scalar
            def _(e):
                run_stream(e, "scalar")
        return nc, es, p

    p.scope = "load"
    DMA("sync", lambda e: e.dma_start(out=idb, in_=idb_d[:]), writes=["idb_s"])
    DMA("sync", lambda e: e.dma_start(out=mask2, in_=mask2_d[:]), writes=["mask2_s"])
    DMA("sync", lambda e: e.dma_start(out=iotmb, in_=iotmb_d[:]), writes=["iotmb_s"])
    DMA("sync", lambda e: e.dma_start(out=pm_s, in_=pmask_d[:]), writes=["pm_s"])
    DMA("sync", lambda e: e.dma_start(out=ebias_s, in_=ebias_d[:]), writes=["ebias_s"])
    DMA("gpsimd", lambda e: e.dma_start(out=xbf.rearrange("q (n h) -> q n h", h=D),
                                        in_=x_d.rearrange("(n q) h -> q n h", q=128)), writes=["xbf"])
    DMA("gpsimd", lambda e: e.dma_start(out=gw_s, in_=gwT_d), writes=["gw"])
    E("vector", lambda e: e.memset(ones_bf, 1.0), writes=["ones_s"])
    E("vector", lambda e: e.memset(eps_s, 1e-5), writes=["eps_s"])
    E("vector", lambda e: e.memset(nbig_s, -BIGV), writes=["nbig_s"])
    E("vector", lambda e: e.memset(tiny_s, 1e-20), writes=["tiny_s"])
    for pr in range(NPAIR):
        E("vector", lambda e, pr=pr: e.memset(
            Vaug8[pr].rearrange("q (j h c) -> q j h c", h=2, c=65)[:, :, :, 64:65], 1.0),
            writes=["Va8_%d" % pr])
    E("vector", lambda e: e.memset(Qm[0:64, :], 0.0), writes=["Qm"])
    E("vector", lambda e: e.memset(scanA[:, 0:65], 0.0), writes=["scanA"])

    def tile_transposes(n, src_reg):
        for c2 in range(4):
            k = 4 + ((n * 4 + c2) % 2)
            EG("tensor", [
                lambda e, n=n, c2=c2, k=k: e.transpose(
                    pbb(k)[:, 0:128], xbf[:, 1024 * n + 256 * c2: 1024 * n + 256 * c2 + 128], idb),
                lambda e, n=n, c2=c2, k=k: e.transpose(
                    pbb(k)[:, 128:256], xbf[:, 1024 * n + 256 * c2 + 128: 1024 * n + 256 * c2 + 256], idb)],
                reads=[src_reg, "idb_s"], writes=["pb%d" % k])
            E("vector", lambda e, n=n, c2=c2, k=k: e.tensor_copy(
                xf.rearrange("q (c r) -> q c r", r=2048)[:, 2 * c2:2 * c2 + 2, 128 * n:128 * (n + 1)],
                pbb(k)[:, 0:256].rearrange("q (a r) -> q a r", a=2)),
                reads=["pb%d" % k], writes=["xf"])

    p.scope = "xT"
    for n in range(NT):
        tile_transposes(n, "xbf")

    if debug:
        for n in range(16):
            DMA("sync", lambda e, n=n: e.dma_start(out=xf_dbg[:, 1024 * n:1024 * (n + 1)],
                                                   in_=xf[:, 1024 * n:1024 * (n + 1)]), reads=["xf"])
    if upto <= 1:
        return _finalize()

    wun = [0]

    def load_wslice(wd, bd, pair):
        u = wun[0]; wun[0] += 1
        slot = u % 4
        DMA("gpsimd", lambda e, wd=wd, pair=pair, slot=slot: e.dma_start(
            out=wring[slot], in_=wd[pair]),
            writes=["wr%d" % slot])
        DMA("gpsimd", lambda e, bd=bd, pair=pair, slot=slot: e.dma_start(
            out=bring[slot], in_=bd[:, 128 * pair:128 * (pair + 1)]),
            writes=["br%d" % slot])
        return slot

    def qkv_fmajor(slot, dst, dst_reg, softplus):
        for u in range(4):
            bank = u
            fns = [lambda e, u=u, c=c, slot=slot, bank=bank: e.matmul(
                pbf(bank), wring[slot].rearrange("q (c o) -> q c o", o=128)[:, c, :],
                xf[:, 2048 * c + 512 * u: 2048 * c + 512 * (u + 1)],
                start=(c == 0), stop=False) for c in range(8)]
            fns.append(lambda e, bank=bank, slot=slot: e.matmul(
                pbf(bank), bring[slot], ones_bf, start=False, stop=True))
            EG("tensor", fns,
               reads=["wr%d" % slot, "xf", "br%d" % slot, "ones_s"], writes=["pb%d" % bank])
            if softplus:
                E("scalar", lambda e, u=u, bank=bank, dst=dst: e.activation(
                    gs2[bank % 2], pbf(bank), AF.Exp),
                    reads=["pb%d" % bank], writes=["gs%d" % (bank % 2)])
                E("scalar", lambda e, u=u, bank=bank, dst=dst, dst_reg=dst_reg: e.activation(
                    dst[:, 512 * u:512 * (u + 1)], gs2[bank % 2], AF.Ln, bias=1.0),
                    reads=["gs%d" % (bank % 2)], writes=[dst_reg])
            else:
                E("vector", lambda e, u=u, bank=bank, dst=dst, dst_reg=dst_reg: e.tensor_copy(
                    dst[:, 512 * u:512 * (u + 1)], pbf(bank)),
                    reads=["pb%d" % bank], writes=[dst_reg])

    for pair in range(NPAIR):
        p.scope = "A%d" % pair
        sk = load_wslice(wkT_d, bk_d, pair)
        sv = load_wslice(wvT_d, bv_d, pair)
        qkv_fmajor(sk, Kf8[pair], "Kf8_%d" % pair, softplus=True)
        for jj in range(NCH // 2):
            k = 4 + (jj % 4)
            EG("tensor", [
                lambda e, jj=jj, k=k, pair=pair: e.transpose(
                    pbb(k)[:, 0:128], Kf8[pair][:, 256 * jj:256 * jj + 128], idb),
                lambda e, jj=jj, k=k, pair=pair: e.transpose(
                    pbb(k)[:, 128:256], Kf8[pair][:, 256 * jj + 128:256 * jj + 256], idb)],
                reads=["Kf8_%d" % pair, "idb_s"], writes=["pb%d" % k])
            E("vector", lambda e, jj=jj, k=k: e.tensor_copy(
                Ktok[:, 256 * jj:256 * (jj + 1)], pbb(k)[:, 0:256]),
                reads=["pb%d" % k], writes=["Ktok"])
        for jj in range(NCH // 2):
            bank = jj % 4
            fns = []
            for sub in range(2):
                j = 2 * jj + sub
                fns += [lambda e, j=j, c=c, sub=sub, slot=sv, bank=bank: e.matmul(
                    pbf(bank)[:, 128 * sub:128 * (sub + 1)],
                    xf[:, 2048 * c + 128 * j: 2048 * c + 128 * (j + 1)],
                    wring[slot].rearrange("q (c o) -> q c o", o=128)[:, c, :],
                    start=(c == 0), stop=False) for c in range(8)]
                fns.append(lambda e, sub=sub, slot=sv, bank=bank: e.matmul(
                    pbf(bank)[:, 128 * sub:128 * (sub + 1)],
                    ones_bf[:, 0:128], bring[slot], start=False, stop=True))
            EG("tensor", fns,
               reads=["wr%d" % sv, "xf", "br%d" % sv, "ones_s"], writes=["pb%d" % bank])
            E("vector", lambda e, jj=jj, bank=bank, pair=pair: e.tensor_copy(
                Vaug8[pair].rearrange("q (j h c) -> q j h c", h=2, c=65)[:, 2 * jj:2 * jj + 2, :, 0:64],
                pbf(bank)[:, 0:256].rearrange("q (a h c) -> q a h c", a=2, h=2)),
                reads=["pb%d" % bank], writes=["Va8_%d" % pair])
        for jj in range(NCH // 2):
            bank = 4 + (jj % 4)
            fns = []
            for sub in range(2):
                j = 2 * jj + sub
                fns += [lambda e, j=j, sub=sub, half=half, bank=bank, pair=pair: e.matmul(
                    pbf(bank)[64 * half:64 * (half + 1), 136 * sub:136 * sub + 65],
                    Ktok[:, 128 * j + 64 * half: 128 * j + 64 * (half + 1)],
                    Vaug8[pair][:, 130 * j + 65 * half: 130 * j + 65 * (half + 1)],
                    start=True, stop=True) for half in range(2)]
            EG("tensor", fns, reads=["Ktok", "Va8_%d" % pair], writes=["pb%d" % bank])
            E("vector", lambda e, jj=jj, bank=bank: e.tensor_copy(
                scanA[:, 65 + 130 * jj:65 + 130 * (jj + 1)].rearrange("q (a c) -> q a c", a=2),
                pbf(bank)[:, 0:272].rearrange("q (a c) -> q a c", c=136)[:, :, 0:65]),
                reads=["pb%d" % bank], writes=["scanA"])
        EG("tensor", [lambda e, j=j, half=half, pair=pair: e.matmul(
            pbf(0)[64 * half:64 * (half + 1), 0:65],
            Ktok[:, 128 * j + 64 * half: 128 * j + 64 * (half + 1)],
            Vaug8[pair][:, 130 * j + 65 * half: 130 * j + 65 * (half + 1)],
            start=(j == 0), stop=(j == NCH - 1))
            for half in range(2) for j in range(NCH)],
            reads=["Ktok", "Va8_%d" % pair], writes=["pb0"])
        E("vector", lambda e: e.tensor_copy(Sfp, pbf(0)[:, 0:65]),
          reads=["pb0"], writes=["Sfp"])
        a3 = lambda ap, nb: ap.rearrange("q (j c) -> q j c", c=65)
        for kk in range(4):
            step = 1 << kk
            E("vector", lambda e, step=step: e.tensor_copy(
                scanB[:, 0:65 * step], scanA[:, 65:65 * (step + 1)]),
                reads=["scanA"], writes=["scanB"])
            E("vector", lambda e, step=step: e.tensor_tensor(
                scanB[:, 65 * step:1040], scanA[:, 65 * (step + 1):1105],
                scanA[:, 65:65 * (17 - step)], ALU.add),
                reads=["scanA"], writes=["scanB"])
            E("vector", lambda e: e.tensor_copy(scanA[:, 65:1105], scanB),
              reads=["scanB"], writes=["scanA"])
        DMA("sync", lambda e, pair=pair: e.dma_start(out=sloc_d[pair], in_=scanA[:, 0:1040]),
            reads=["scanA"], writes=["sloc_hbm"])
        for half in range(2):
            DMA("sync", lambda e, pair=pair, half=half: e.dma_start(
                out=cc_in.ap()[:, 65 * (2 * pair + half): 65 * (2 * pair + half + 1)],
                in_=view(b_Sfp, F32, 65, part=64, poff=64 * half)),
                reads=["Sfp"], writes=["cc_in"])

    if debug:
        DMA("sync", lambda e: e.dma_start(out=ccin_dbg[:], in_=cc_in.ap()[:]), reads=["cc_in"])
    if upto <= 2:
        return _finalize()

    qslot = {}

    def phaseC_load(pair):
        qslot[pair] = load_wslice(wqT_d, bq_d, pair)

    def phaseC_part1(pair):
        p.scope = "C1_%d" % pair
        if pair not in qslot:
            phaseC_load(pair)
        sq_ = qslot[pair]
        qkv_fmajor(sq_, Qf, "Qf", softplus=True)
        E("vector", lambda e: e.tensor_copy(Qm[64:128, :], Qf[64:128, :]),
          reads=["Qf"], writes=["Qm"])

    def phaseC_part2(pair):
        p.scope = "C2_%d" % pair
        Kfp = Kf8[pair]
        Vap = Vaug8[pair]
        rKf = "Kf8_%d" % pair
        rVa = "Va8_%d" % pair
        slr = "SlR%d" % (pair % 2)
        if pair + 1 < NPAIR:
            DMA("sync", lambda e, pair=pair: e.dma_start(
                out=SlocR[(pair + 1) % 2], in_=sloc_d[pair + 1]),
                reads=["sloc_hbm"], writes=["SlR%d" % ((pair + 1) % 2)])
        E("vector", lambda e, pair=pair, slr=slr: e.tensor_tensor(
            Saug.rearrange("q (j c) -> q j c", c=65),
            SlocR[pair % 2].rearrange("q (j c) -> q j c", c=65),
            Sprepb[:, 65 * pair:65 * (pair + 1)].rearrange("q (o c) -> q o c", o=1).broadcast_to((128, 16, 65)),
            ALU.add),
            reads=[slr, "Sprepb"], writes=["Sg%d" % _j for _j in range(16)])

        def stage1(uu):
            ab = uu % 2
            fns = []
            for sub in range(2):
                j = 2 * uu + sub
                fns += [
                    lambda e, j=j, sub=sub, ab=ab: e.matmul(
                        pbf(ab)[:, 256 * sub:256 * sub + 128],
                        Kfp[0:64, 128 * j:128 * (j + 1)],
                        Qf[0:64, 128 * j:128 * (j + 1)],
                        start=True, stop=True),
                    lambda e, j=j, sub=sub, ab=ab: e.matmul(
                        pbf(ab)[:, 256 * sub + 128:256 * sub + 256],
                        Kfp[:, 128 * j:128 * (j + 1)],
                        Qm[:, 128 * j:128 * (j + 1)],
                        start=True, stop=True)]
            EG("tensor", fns, reads=[rKf, "Qf", "Qm"], writes=["pb%d" % ab])
            E("vector", lambda e, uu=uu, ab=ab: e.tensor_tensor(
                Mp2[uu % 2], pbf(ab)[:, 0:512], mask2, ALU.mult),
                reads=["pb%d" % ab, "mask2_s"], writes=["Mp%d" % (uu % 2)])

        def stage2(uu):
            nb = 2 + (uu % 2)
            fns = []
            for sub in range(2):
                j = 2 * uu + sub
                co0 = 136 * sub
                fns += [
                    lambda e, j=j, sub=sub, nb=nb, co0=co0: e.matmul(
                        pbf(nb)[:, co0:co0 + 65], Mp2[(j // 2) % 2][:, 256 * sub:256 * sub + 128],
                        Vap[:, 130 * j:130 * j + 65],
                        start=True, stop=False),
                    lambda e, j=j, nb=nb, co0=co0: e.matmul(
                        pbf(nb)[:, co0:co0 + 65], Qf[0:64, 128 * j:128 * (j + 1)],
                        Saug[0:64, 65 * j:65 * (j + 1)],
                        start=False, stop=True),
                    lambda e, j=j, sub=sub, nb=nb, co0=co0: e.matmul(
                        pbf(nb)[:, co0 + 68:co0 + 133], Mp2[(j // 2) % 2][:, 256 * sub + 128:256 * sub + 256],
                        Vap[:, 130 * j + 65:130 * j + 130],
                        start=True, stop=False),
                    lambda e, j=j, nb=nb, co0=co0: e.matmul(
                        pbf(nb)[:, co0 + 68:co0 + 133], Qm[:, 128 * j:128 * (j + 1)],
                        Saug[:, 65 * j:65 * (j + 1)],
                        start=False, stop=True)]
            EG("tensor", fns,
               reads=["Mp%d" % (uu % 2), rVa, "Qf", "Qm",
                      "Sg%d" % (2 * uu), "Sg%d" % (2 * uu + 1)],
               writes=["pb%d" % nb])
            E("vector", lambda e, uu=uu, nb=nb: e.reciprocal(
                rden2[uu % 2],
                pbf(nb)[:, 0:272].rearrange("q (h c) -> q h c", c=68)[:, :, 64]),
                reads=["pb%d" % nb], writes=["rden%d" % (uu % 2)])
            for sub in range(2):
                for half in range(2):
                    co = 136 * sub + 68 * half
                    E("scalar", lambda e, uu=uu, sub=sub, half=half, nb=nb, co=co: e.activation(
                        otb2[uu % 2][:, 128 * sub + 64 * half:128 * sub + 64 * (half + 1)],
                        pbf(nb)[:, co:co + 64], AF.Copy,
                        scale=rden2[uu % 2][:, 2 * sub + half:2 * sub + half + 1]),
                        reads=["pb%d" % nb, "rden%d" % (uu % 2)], writes=["otb%d" % (uu % 2)])

        def stage3(uu):
            tb = 4 + (uu % 2)
            EG("tensor", [
                lambda e, uu=uu, tb=tb: e.transpose(
                    pbb(tb)[:, 0:128], otb2[uu % 2][:, 0:128], idb),
                lambda e, uu=uu, tb=tb: e.transpose(
                    pbb(tb)[:, 128:256], otb2[uu % 2][:, 128:256], idb)],
                reads=["otb%d" % (uu % 2), "idb_s"], writes=["pb%d" % tb])
            E("vector", lambda e, uu=uu, pair=pair, tb=tb: e.tensor_copy(
                af[:, 2048 * pair + 256 * uu: 2048 * pair + 256 * (uu + 1)], pbb(tb)[:, 0:256]),
                reads=["pb%d" % tb], writes=["af"])

        NU = NCH // 2
        for i in range(NU + 2):
            if i < NU:
                stage1(i)
            if 0 <= i - 1 < NU:
                stage2(i - 1)
            if 0 <= i - 2 < NU:
                stage3(i - 2)

    if upto >= 4:
        phaseC_part1(0)
        DMA("sync", lambda e: e.dma_start(out=SlocR[0], in_=sloc_d[0]),
            reads=["sloc_hbm"], writes=["SlR0"])

    p.scope = "coll"
    E("gpsimd", lambda e: e.collective_compute(
        "AllGather", ALU.bypass, replica_groups=[list(range(NCORE))],
        ins=[cc_in.ap()], outs=[cc_out.ap()]),
        reads=["cc_in"], writes=["cc_out"], sem="cc", amount=1)
    E("vector", lambda e: e.memset(Sprep, 0.0), writes=["Sprep"])
    for r in range(NCORE):
        for t in range(2):
            DMA("sync", lambda e, r=r, t=t: e.dma_start(
                out=gath[64 * t:64 * (t + 1), :].rearrange("q (pp c) -> q pp c", c=65),
                in_=cc_out.ap()[64 * r:64 * (r + 1), :].rearrange(
                    "q (pp two c) -> q two pp c", two=2, c=65)[:, t]),
                reads=["cc_out"], writes=["gath"])
        E("vector", lambda e, r=r: e.scalar_tensor_tensor(
            Sprep, gath, pm_s[:, r:r + 1], Sprep, ALU.mult, ALU.add),
            reads=["gath", "pm_s", "Sprep"], writes=["Sprep"])
    E("scalar", lambda e: e.copy(Sprepb, Sprep), reads=["Sprep"], writes=["Sprepb"])

    if debug:
        DMA("sync", lambda e: e.dma_start(out=sprep_dbg[:], in_=Sprep), reads=["Sprep"])
    if upto <= 3:
        return _finalize()

    phaseC_load(1)
    phaseC_part2(0)
    for pair in range(1, NPAIR):
        phaseC_part1(pair)
        if pair + 1 < NPAIR:
            phaseC_load(pair + 1)
        phaseC_part2(pair)
        if pair == 3:
            p.scope = "C2_%d" % pair
            DMA("gpsimd", lambda e: e.dma_start(out=bo_s, in_=bo_d[:]), writes=["bo_s"])
            DMA("gpsimd", lambda e: e.dma_start(out=wo_s, in_=woT_d), writes=["wo"])

    if debug:
        for n in range(16):
            DMA("sync", lambda e, n=n: e.dma_start(out=af_dbg[:, 1024 * n:1024 * (n + 1)],
                                                   in_=af[:, 1024 * n:1024 * (n + 1)]), reads=["af"])
    if upto <= 4:
        return _finalize()

    p.scope = "D"
    for n in range(NT):
        par = n % 2
        for oh in range(2):
            bank = (n * 2 + oh) % 4
            fns = [lambda e, n=n, oh=oh, c=c, bank=bank: e.matmul(
                pbf(bank), af[:, 2048 * c + 128 * n:2048 * c + 128 * (n + 1)],
                wo_s.rearrange("q (c o) -> q c o", o=1024)[:, c, 512 * oh:512 * (oh + 1)],
                start=(c == 0), stop=False) for c in range(8)]
            fns.append(lambda e, oh=oh, bank=bank: e.matmul(
                pbf(bank), ones_bf[:, 0:128], bo_s[:, 512 * oh:512 * (oh + 1)],
                start=False, stop=True))
            EG("tensor", fns, reads=["af", "wo", "ones_s", "bo_s"], writes=["pb%d" % bank])
        for oh in range(2):
            bank = (n * 2 + oh) % 4
            E("vector", lambda e, n=n, oh=oh, bank=bank, par=par: e.scalar_tensor_tensor(
                xt32[par][:, 512 * oh:512 * (oh + 1)], pbf(bank),
                eps_s, xbf[:, 1024 * n + 512 * oh:1024 * n + 512 * (oh + 1)],
                ALU.bypass, ALU.add, accum_out=mst[:, 4 * n + oh:4 * n + oh + 1]),
                reads=["pb%d" % bank, "xbf", "eps_s"], writes=["xt%d" % par, "mst"])
        E("vector", lambda e, n=n: e.tensor_tensor(
            mst[:, 4 * n:4 * n + 1], mst[:, 4 * n:4 * n + 1], mst[:, 4 * n + 1:4 * n + 2], ALU.add),
            reads=["mst"], writes=["mst"])
        E("scalar", lambda e, n=n: e.mul(mst[:, 4 * n + 1:4 * n + 2], mst[:, 4 * n:4 * n + 1], -1.0 / D),
          reads=["mst"], writes=["mst"])
        E("scalar", lambda e, n=n, par=par: e.activation(
            xsq2[par], xt32[par], AF.Square,
            bias=mst[:, 4 * n + 1:4 * n + 2], accum_out=mst[:, 4 * n + 2:4 * n + 3]),
            reads=["xt%d" % par, "mst"], writes=["xsq%d" % par, "mst"])
        E("scalar", lambda e, n=n: e.activation(
            mst[:, 4 * n + 2:4 * n + 3], mst[:, 4 * n + 2:4 * n + 3], AF.Ln,
            scale=1.0 / D, bias=eps_s),
            reads=["mst", "eps_s"], writes=["mst"])
        E("scalar", lambda e, n=n: e.activation(
            mst[:, 4 * n + 3:4 * n + 4], mst[:, 4 * n + 2:4 * n + 3], AF.Exp, scale=-0.5),
            reads=["mst"], writes=["mst"])
        E("vector", lambda e, n=n, par=par: e.tensor_scalar(
            xbf[:, 1024 * n:1024 * (n + 1)], xt32[par],
            mst[:, 4 * n + 1:4 * n + 2], mst[:, 4 * n + 3:4 * n + 4], ALU.add, ALU.mult),
            reads=["xt%d" % par, "mst"], writes=["xbf"])
        tile_transposes(n, "xbf")

    if debug:
        for n in range(16):
            DMA("sync", lambda e, n=n: e.dma_start(out=x1_dbg[:, 1024 * n:1024 * (n + 1)],
                                                   in_=xbf[:, 1024 * n:1024 * (n + 1)]), reads=["xbf"])
    if upto <= 5:
        return _finalize()

    p.scope = "R"
    for n in range(NT):
        bank = 6 + (n % 2)
        EG("tensor", [lambda e, n=n, c=c, bank=bank: e.matmul(
            pbf(bank)[:, 0:16], xf[:, 2048 * c + 128 * n:2048 * c + 128 * (n + 1)],
            gw_s.rearrange("q (c e) -> q c e", e=16)[:, c, :],
            start=(c == 0), stop=(c == 7)) for c in range(8)],
            reads=["xf", "gw"], writes=["pb%d" % bank])
        E("scalar", lambda e, n=n, bank=bank: e.activation(scores[:, 16 * n:16 * (n + 1)], pbf(bank)[:, 0:16], AF.Sigmoid),
          reads=["pb%d" % bank], writes=["scores"])
    E("vector", lambda e: e.tensor_tensor(
        tmpr.rearrange("q (t e) -> q t e", e=16), scores.rearrange("q (t e) -> q t e", e=16),
        ebias_s.rearrange("q (o e) -> q o e", o=1).broadcast_to((128, 16, 16)),
        ALU.add), reads=["scores", "ebias_s"], writes=["tmpr"])
    pairs = [(0, 1), (0, 2), (0, 3), (1, 2), (1, 3), (2, 3)]
    for idx, (i, jj) in enumerate(pairs):
        if idx == 0:
            E("vector", lambda e, i=i, jj=jj: e.tensor_tensor(
                gall, tmpr[:, i::4], tmpr[:, jj::4], ALU.add), reads=["tmpr"], writes=["gall"])
        else:
            E("vector", lambda e, i=i, jj=jj: e.tensor_tensor(
                eqb[:, 0:64], tmpr[:, i::4], tmpr[:, jj::4], ALU.add), reads=["tmpr"], writes=["eqb"])
            E("vector", lambda e: e.tensor_tensor(gall, gall, eqb[:, 0:64], ALU.max),
              reads=["gall", "eqb"], writes=["gall"])
    for idx, (i, jj) in enumerate(pairs):
        if idx == 0:
            E("vector", lambda e, i=i, jj=jj: e.tensor_tensor(
                tau, gall[:, i::4], gall[:, jj::4], ALU.min), reads=["gall"], writes=["tau"])
        else:
            E("vector", lambda e, i=i, jj=jj: e.tensor_tensor(
                rsc[:, 0:16], gall[:, i::4], gall[:, jj::4], ALU.min), reads=["gall"], writes=["rsc"])
            E("vector", lambda e: e.tensor_tensor(tau, tau, rsc[:, 0:16], ALU.max),
              reads=["tau", "rsc"], writes=["tau"])
    for g in range(4):
        E("vector", lambda e, g=g: e.tensor_tensor(gm[:, g::4], gall[:, g::4], tau, ALU.is_ge),
          reads=["gall", "tau"], writes=["gm"])
    for i in range(4):
        E("vector", lambda e, i=i: e.tensor_tensor(tmpr[:, i::4], tmpr[:, i::4], gm, ALU.mult),
          reads=["tmpr", "gm"], writes=["tmpr"])
    E("vector", lambda e: e.memset(comb, 0.0), writes=["comb"])
    t3 = lambda ap: ap.rearrange("q (t e) -> q t e", e=16)
    for rnd in range(4):
        E("vector", lambda e: e.reduce_max(rsc[:, 16:32].rearrange("q (t e) -> q t e", e=1), t3(tmpr), axis=AX.X),
          reads=["tmpr"], writes=["rsc"])
        E("vector", lambda e: e.tensor_tensor(
            t3(eqb), t3(tmpr), rsc[:, 16:32].rearrange("q (t e) -> q t e", e=1).broadcast_to((128, 16, 16)), ALU.is_ge),
            reads=["tmpr", "rsc"], writes=["eqb"])
        E("vector", lambda e: e.tensor_tensor(eqb, eqb, iotmb, ALU.mult),
          reads=["eqb", "iotmb_s"], writes=["eqb"])
        E("vector", lambda e: e.tensor_reduce(
            rsc[:, 16:32].rearrange("q (t e) -> q t e", e=1), t3(eqb), AX.X, ALU.min),
            reads=["eqb"], writes=["rsc"])
        E("vector", lambda e: e.tensor_tensor(
            t3(eqb), t3(iotmb), rsc[:, 16:32].rearrange("q (t e) -> q t e", e=1).broadcast_to((128, 16, 16)), ALU.is_equal),
            reads=["iotmb_s", "rsc"], writes=["eqb"])
        E("vector", lambda e: e.tensor_tensor(comb, comb, eqb, ALU.add),
          reads=["comb", "eqb"], writes=["comb"])
        if rnd < 3:
            E("vector", lambda e: e.scalar_tensor_tensor(
                tmpr, eqb, nbig_s, tmpr, ALU.mult, ALU.add),
                reads=["eqb", "nbig_s", "tmpr"], writes=["tmpr"])
    E("vector", lambda e: e.tensor_tensor(comb, comb, scores, ALU.mult),
      reads=["comb", "scores"], writes=["comb"])
    E("vector", lambda e: e.reduce_sum(
        rsc[:, 16:32].rearrange("q (t e) -> q t e", e=1), t3(comb), axis=AX.X),
        reads=["comb"], writes=["rsc"])
    E("vector", lambda e: e.tensor_scalar(rsc[:, 16:32], rsc[:, 16:32], tiny_s, None, ALU.add),
      reads=["rsc", "tiny_s"], writes=["rsc"])
    E("vector", lambda e: e.reciprocal(rsc[:, 0:16], rsc[:, 16:32]),
      reads=["rsc"], writes=["rsc"])
    E("vector", lambda e: e.tensor_tensor(
        t3(comb), t3(comb), rsc[:, 0:16].rearrange("q (t e) -> q t e", e=1).broadcast_to((128, 16, 16)), ALU.mult),
        reads=["comb", "rsc"], writes=["comb"])

    if debug:
        DMA("sync", lambda e: e.dma_start(out=comb_dbg[:], in_=comb), reads=["comb"])
    if upto <= 6:
        return _finalize()

    eun = [0]

    def load_eunit(which, e_idx):
        u = eun[0]; eun[0] += 1
        slot = u % 6
        srcs = (gateT_d, upT_d, downT_d)[which]
        DMA("gpsimd", lambda e, srcs=srcs, e_idx=e_idx, slot=slot: e.dma_start(
            out=ers[slot], in_=srcs[e_idx]),
            writes=["er%d" % slot])
        return slot

    for hf in range(2):
        p.scope = "FFN%d" % hf
        for ei in range(NEXP):
            sgate = load_eunit(0, ei)
            sup = load_eunit(1, ei)
            sdown = load_eunit(2, ei)
            for ic in range(4):
                for u in range(2):
                    it = ic * 2 + u
                    gbank = it % 2
                    ubank = 2 + it % 2
                    gsp = gs2[it % 2]
                    EG("tensor", [lambda e, ic=ic, u=u, c=c, gbank=gbank, sgate=sgate, hf=hf: e.matmul(
                        pbf(gbank), ers[sgate].rearrange("q (c i) -> q c i", c=8)[:, c, 128 * ic:128 * (ic + 1)],
                        xf[:, 2048 * c + 1024 * hf + 512 * u: 2048 * c + 1024 * hf + 512 * (u + 1)],
                        start=(c == 0), stop=(c == 7)) for c in range(8)],
                        reads=["er%d" % sgate, "xf"], writes=["pb%d" % gbank])
                    EG("tensor", [lambda e, ic=ic, u=u, c=c, ubank=ubank, sup=sup, hf=hf: e.matmul(
                        pbf(ubank), ers[sup].rearrange("q (c i) -> q c i", c=8)[:, c, 128 * ic:128 * (ic + 1)],
                        xf[:, 2048 * c + 1024 * hf + 512 * u: 2048 * c + 1024 * hf + 512 * (u + 1)],
                        start=(c == 0), stop=(c == 7)) for c in range(8)],
                        reads=["er%d" % sup, "xf"], writes=["pb%d" % ubank])
                    E("scalar", lambda e, gbank=gbank, gsp=gsp: e.activation(gsp, pbf(gbank), AF.Silu),
                      reads=["pb%d" % gbank], writes=["gs%d" % (it % 2)])
                    E("vector", lambda e, ic=ic, u=u, ubank=ubank, gsp=gsp: e.tensor_tensor(
                        actb[:, 1024 * ic + 512 * u: 1024 * ic + 512 * (u + 1)], gsp, pbf(ubank), ALU.mult),
                        reads=["gs%d" % (it % 2), "pb%d" % ubank], writes=["act%d_%d" % (ic, u)])
            for j in range(8):
                nglob = 8 * hf + j
                for oh in range(2):
                    bank = 4 + (j * 2 + oh) % 4
                    EG("tensor", [lambda e, j=j, oh=oh, ic=ic, bank=bank, sdown=sdown: e.matmul(
                        pbf(bank), actb[:, 1024 * ic + 128 * j: 1024 * ic + 128 * (j + 1)],
                        ers[sdown].rearrange("q (c o) -> q c o", c=4)[:, ic, 512 * oh:512 * (oh + 1)],
                        start=(ic == 0), stop=(ic == 3)) for ic in range(4)],
                        reads=["act%d_%d" % (ic2, j // 4) for ic2 in range(4)] + ["er%d" % sdown],
                        writes=["pb%d" % bank])
                    if ei == 0:
                        E("vector", lambda e, j=j, oh=oh, bank=bank, nglob=nglob: e.tensor_scalar(
                            yv[:, 1024 * j + 512 * oh:1024 * j + 512 * (oh + 1)], pbf(bank),
                            comb[:, 16 * nglob + 0:16 * nglob + 1], None, ALU.mult),
                            reads=["pb%d" % bank, "comb"], writes=["yv"])
                    else:
                        E("vector", lambda e, j=j, oh=oh, bank=bank, nglob=nglob, ei=ei: e.scalar_tensor_tensor(
                            yv[:, 1024 * j + 512 * oh:1024 * j + 512 * (oh + 1)], pbf(bank),
                            comb[:, 16 * nglob + ei:16 * nglob + ei + 1],
                            yv[:, 1024 * j + 512 * oh:1024 * j + 512 * (oh + 1)], ALU.mult, ALU.add),
                            reads=["pb%d" % bank, "comb", "yv"], writes=["yv"])
        p.scope = "LN2_%d" % hf
        for j in range(8):
            n = 8 * hf + j
            mb = 64 + 4 * n
            par = n % 2
            E("vector", lambda e, j=j, n=n, mb=mb, par=par: e.scalar_tensor_tensor(
                xsq2[par], yv[:, 1024 * j:1024 * (j + 1)], eps_s,
                xbf[:, 1024 * n:1024 * (n + 1)], ALU.bypass, ALU.add,
                accum_out=mst[:, mb:mb + 1]),
                reads=["yv", "xbf", "eps_s"], writes=["xsq%d" % par, "mst"])
            E("scalar", lambda e, mb=mb: e.mul(mst[:, mb + 1:mb + 2], mst[:, mb:mb + 1], -1.0 / D),
              reads=["mst"], writes=["mst"])
            E("scalar", lambda e, mb=mb, par=par: e.activation(
                xsqB[:, 0:1024], xsq2[par], AF.Square, bias=mst[:, mb + 1:mb + 2], accum_out=mst[:, mb + 2:mb + 3]),
                reads=["xsq%d" % par, "mst"], writes=["xsqB", "mst"])
            E("scalar", lambda e, mb=mb: e.activation(
                mst[:, mb + 2:mb + 3], mst[:, mb + 2:mb + 3], AF.Ln, scale=1.0 / D, bias=eps_s),
                reads=["mst", "eps_s"], writes=["mst"])
            E("scalar", lambda e, mb=mb: e.activation(mst[:, mb + 3:mb + 4], mst[:, mb + 2:mb + 3], AF.Exp, scale=-0.5),
              reads=["mst"], writes=["mst"])
            E("vector", lambda e, mb=mb, par=par: e.tensor_scalar(
                xsqB[:, 0:1024], xsq2[par], mst[:, mb + 1:mb + 2], mst[:, mb + 3:mb + 4], ALU.add, ALU.mult),
                reads=["xsq%d" % par, "mst"], writes=["xsqB"])
            DMA("sync", lambda e, n=n: e.dma_start(
                out=out_d[128 * n:128 * (n + 1), :], in_=xsqB[:, 0:1024]),
                reads=["xsqB"], writes=["outd"])

    return _finalize()


_CACHE = {}


def make_in_maps(inputs):
    x = np.asarray(inputs["L0_attn_x"], dtype=np.float32).reshape(S, D)
    wq = np.asarray(inputs["L0_attn_Wq"], dtype=np.float32)
    wk = np.asarray(inputs["L0_attn_Wk"], dtype=np.float32)
    wv = np.asarray(inputs["L0_attn_Wv"], dtype=np.float32)
    wo = np.asarray(inputs["L0_attn_Wo"], dtype=np.float32)
    bq = np.asarray(inputs["L0_attn_bq"], dtype=np.float32).reshape(1, D)
    bk = np.asarray(inputs["L0_attn_bk"], dtype=np.float32).reshape(1, D)
    bv = np.asarray(inputs["L0_attn_bv"], dtype=np.float32).reshape(1, D)
    bo = np.asarray(inputs["L0_attn_bo"], dtype=np.float32).reshape(1, D)
    gw = np.asarray(inputs["L0_ffn_gate_weight"], dtype=np.float32)
    ebias = np.asarray(inputs["L0_ffn_e_bias"], dtype=np.float32)
    gate = np.asarray(inputs["L0_ffn_gate_proj"], dtype=np.float32)
    up = np.asarray(inputs["L0_ffn_up_proj"], dtype=np.float32)
    down = np.asarray(inputs["L0_ffn_down_proj"], dtype=np.float32)

    def _sw_qkv(w):
        return np.ascontiguousarray(w.T.reshape(8, 128, 8, 128).transpose(2, 1, 0, 3).reshape(8, 128, 1024))
    wqT = _sw_qkv(wq)
    wkT = _sw_qkv(wk)
    wvT = _sw_qkv(wv)
    woT = np.ascontiguousarray(wo.T.reshape(8, 128, 1024).transpose(1, 0, 2).reshape(128, 8192))
    gwT = np.ascontiguousarray(gw.T.reshape(8, 128, 16).transpose(1, 0, 2).reshape(128, 128))
    gateT = np.ascontiguousarray(gate.transpose(0, 2, 1).reshape(16, 8, 128, 512).transpose(0, 2, 1, 3).reshape(16, 128, 4096))
    upT = np.ascontiguousarray(up.transpose(0, 2, 1).reshape(16, 8, 128, 512).transpose(0, 2, 1, 3).reshape(16, 128, 4096))
    downT = np.ascontiguousarray(down.transpose(0, 2, 1).reshape(16, 4, 128, 1024).transpose(0, 2, 1, 3).reshape(16, 128, 4096))
    ebias_t = np.ascontiguousarray(np.broadcast_to(ebias.reshape(1, NEXP), (128, NEXP)))
    idb = np.eye(128, dtype=ml_dtypes.bfloat16)
    m = np.triu(np.ones((128, 128), np.float32))
    mask2 = np.concatenate([m, m, m, m], axis=1)
    iotmb = np.ascontiguousarray(np.broadcast_to(np.tile(np.arange(16, dtype=np.float32), 16)[None, :], (128, 256)) - BIGV)

    in_maps = []
    for c in range(NCORE):
        pmask = np.zeros((128, NCORE), np.float32)
        pmask[:, :c] = 1.0
        in_maps.append({
            "x": np.ascontiguousarray(x[TC * c:TC * (c + 1)]),
            "wqT": wqT, "wkT": wkT, "wvT": wvT, "woT": woT,
            "bq": bq, "bk": bk, "bv": bv, "bo": bo,
            "gwT": gwT, "ebias": ebias_t,
            "gateT": gateT, "upT": upT, "downT": downT,
            "idb": idb, "mask2": mask2,
            "iotmb": iotmb, "pmask": pmask,
        })
    return in_maps


def run_probe(inputs, upto=99, debug=True, trace=False):
    nc, es, prog = build_program(upto=upto, debug=debug)
    print("probe upto=%d: n_ops=%d n_waits=%d" % (upto, prog.n_ops, prog.n_waits))
    in_maps = make_in_maps(inputs)
    res = run_bass_kernel_spmd(nc, in_maps, core_ids=list(range(NCORE)), trace=trace)
    return res


def kernel(**inputs):
    if "nc" not in _CACHE:
        nc, es, prog = build_program()
        _CACHE["nc"] = nc
        _CACHE["es"] = es
    nc = _CACHE["nc"]
    in_maps = make_in_maps(inputs)
    res = run_bass_kernel_spmd(nc, in_maps, core_ids=list(range(NCORE)))
    out = np.concatenate([np.asarray(r["out"], dtype=np.float32) for r in res.results], axis=0)
    return out.reshape(1, S, D)


# version 15
# speedup vs baseline: 1.5513x; 1.2543x over previous
import numpy as np
import ml_dtypes
import concourse.bass as bass
import concourse.mybir as mybir
from concourse.bass_utils import run_bass_kernel_spmd
from contextlib import ExitStack

F32 = mybir.dt.float32
BF16 = mybir.dt.bfloat16
U8 = mybir.dt.uint8
AF = mybir.ActivationFunctionType
ALU = mybir.AluOpType
AX = mybir.AxisListType

NCORE = 8
S = 16384
TC = 2048
D = 1024
NH = 16
DH = 64
NPAIR = 8
NCH = 16
NT = 16
NEXP = 16
DI = 512
BIGV = 1.0e4

ENGINES = ("tensor", "vector", "scalar", "sync", "gpsimd")
CSEM = {"tensor": "st", "vector": "sv", "scalar": "ss", "sync": "sp", "gpsimd": "sg"}
QPOOL = {"sync": 3, "gpsimd": 8}


class Prog:

    def __init__(self):
        self.stream = {e: [] for e in ENGINES}
        self.cnt = {}
        self.reg = {}
        self.waited = {e: {} for e in ENGINES}
        self.iv = {}
        self.overlaps = {}
        self.n_ops = 0
        self.n_waits = 0
        self.scope = "init"

    def interval(self, name, space, lo, hi):
        self.iv[name] = (space, lo, hi)
        self.overlaps = {}

    def _family(self, name):
        got = self.overlaps.get(name)
        if got is not None:
            return got
        if name not in self.iv:
            fam = (name,)
        else:
            sp, lo, hi = self.iv[name]
            fam = tuple(
                o for o, (sp2, lo2, hi2) in self.iv.items()
                if sp2 == sp and lo < hi2 and lo2 < hi
            )
        self.overlaps[name] = fam
        return fam

    def emit(self, eng, fn, reads=(), writes=(), sem=None, amount=1):
        if sem is None:
            sem = CSEM[eng]
        waits = {}

        def need(m):
            if m is None:
                return
            s, v = m
            if s == CSEM[eng] and eng == "tensor":
                return
            if self.waited[eng].get(s, 0) >= v:
                return
            waits[s] = max(waits.get(s, 0), v)

        rfam, wfam = set(), set()
        for r in reads:
            rfam.update(self._family(r))
        for r in writes:
            wfam.update(self._family(r))
        for r in rfam:
            e = self.reg.get(r)
            if e:
                need(e["w"])
        for r in wfam:
            e = self.reg.get(r)
            if e:
                need(e["w"])
                for s, v in e["r"].items():
                    need((s, v))
        for s, v in sorted(waits.items()):
            self.stream[eng].append(("wait", s, v))
            self.waited[eng][s] = max(self.waited[eng].get(s, 0), v)
            self.n_waits += 1
        self.cnt[sem] = self.cnt.get(sem, 0) + amount
        v_after = self.cnt[sem]
        self.stream[eng].append(("op", fn, sem, amount, self.scope))
        self.n_ops += 1
        for r in reads:
            self.reg.setdefault(r, {"w": None, "r": {}})["r"][sem] = v_after
        for r in writes:
            for rr in self._family(r):
                if rr == r:
                    self.reg[r] = {"w": (sem, v_after), "r": {}}
                else:
                    self.reg[rr] = {"w": (sem, v_after), "r": {}}

    def dma(self, eng, fn, reads=(), writes=()):
        pool = QPOOL[eng]
        i = self.cnt.get("dmaidx_" + eng, 0)
        self.cnt["dmaidx_" + eng] = i + 1
        sem = "q%s%d" % (eng[0], i % pool)
        gate = 16 * (i // pool)
        if gate > 0 and self.waited[eng].get(sem, 0) < gate:
            self.stream[eng].append(("wait", sem, gate))
            self.waited[eng][sem] = gate
            self.n_waits += 1
        self.emit(eng, fn, reads, writes, sem=sem, amount=16)


def build_program(upto=99, debug=False, c_pairs=NPAIR, c_step=9):
    nc = bass.Bass("TRN2", target_bir_lowering=False)

    x_d = nc.dram_tensor("x", [TC, D], F32, kind="ExternalInput").ap()
    wqT_d = nc.dram_tensor("wqT", [NPAIR, 128, D], F32, kind="ExternalInput").ap()
    wkT_d = nc.dram_tensor("wkT", [NPAIR, 128, D], F32, kind="ExternalInput").ap()
    wvT_d = nc.dram_tensor("wvT", [NPAIR, 128, D], F32, kind="ExternalInput").ap()
    woT_d = nc.dram_tensor("woT", [128, 8 * D], F32, kind="ExternalInput").ap()
    bqc_d = nc.dram_tensor("bqc", [128, NPAIR], F32, kind="ExternalInput").ap()
    bkc_d = nc.dram_tensor("bkc", [128, NPAIR], F32, kind="ExternalInput").ap()
    bv_d = nc.dram_tensor("bv", [1, D], F32, kind="ExternalInput").ap()
    bo_d = nc.dram_tensor("bo", [1, D], F32, kind="ExternalInput").ap()
    gwT_d = nc.dram_tensor("gwT", [128, 8 * NEXP], F32, kind="ExternalInput").ap()
    ebias_d = nc.dram_tensor("ebias", [128, NEXP], F32, kind="ExternalInput").ap()
    gateT_d = nc.dram_tensor("gateT", [NEXP, 128, 8 * DI], F32, kind="ExternalInput").ap()
    upT_d = nc.dram_tensor("upT", [NEXP, 128, 8 * DI], F32, kind="ExternalInput").ap()
    downT_d = nc.dram_tensor("downT", [NEXP, 128, 4 * D], F32, kind="ExternalInput").ap()
    idb_d = nc.dram_tensor("idb", [128, 128], BF16, kind="ExternalInput").ap()
    mask2_d = nc.dram_tensor("mask2", [128, 512], F32, kind="ExternalInput").ap()
    iotmb_d = nc.dram_tensor("iotmb", [128, 256], F32, kind="ExternalInput").ap()
    pmask_d = nc.dram_tensor("pmask", [128, NCORE], F32, kind="ExternalInput").ap()
    out_d = nc.dram_tensor("out", [TC, D], F32, kind="ExternalOutput").ap()

    cc_in = nc.dram_tensor("cc_in", [64, NH * 65], F32)
    sloc_d = nc.dram_tensor("sloc_d", [NPAIR, 128, NCH * 65], BF16)
    cc_out = nc.dram_tensor("cc_out", [64 * NCORE, NH * 65], F32)
    if debug:
        xf_dbg = nc.dram_tensor("xf_dbg", [128, 16384], BF16, kind="ExternalOutput").ap()
        ccin_dbg = nc.dram_tensor("ccin_dbg", [64, NH * 65], F32, kind="ExternalOutput").ap()
        sprep_dbg = nc.dram_tensor("sprep_dbg", [128, 520], F32, kind="ExternalOutput").ap()
        af_dbg = nc.dram_tensor("af_dbg", [128, 16384], BF16, kind="ExternalOutput").ap()
        x1_dbg = nc.dram_tensor("x1_dbg", [128, 16384], BF16, kind="ExternalOutput").ap()
        comb_dbg = nc.dram_tensor("comb_dbg", [128, 256], F32, kind="ExternalOutput").ap()

    p = Prog()
    es = ExitStack()

    SBUF_BYTES = 212736
    arena = es.enter_context(nc.sbuf_tensor("arena", [128, SBUF_BYTES], U8))
    alloc_off = [0]

    def carve(name, nbytes, align=64):
        off = (alloc_off[0] + align - 1) // align * align
        alloc_off[0] = off + nbytes
        assert alloc_off[0] <= SBUF_BYTES, f"SBUF overflow at {name}: {alloc_off[0]}"
        return name, off

    def view(buf, dtype, cols, part=128, poff=0):
        name, off = buf
        isz = 2 if dtype in (BF16,) else (4 if dtype in (F32,) else 1)
        apv = arena[:].bitcast(dtype)
        ap2 = apv[poff:poff + part, off // isz: off // isz + cols]
        return ap2

    def reg_iv(buf, nbytes, name=None):
        nm, off = buf
        p.interval(name or nm, "sbuf", off, off + nbytes)

    b_xf = carve("xf", 32768); reg_iv(b_xf, 32768)
    b_xbf = carve("xbf", 32768); reg_iv(b_xbf, 32768)
    b_gs = carve("gs", 2 * 2048)
    for _k in range(2):
        p.interval("gs%d" % _k, "sbuf", b_gs[1] + 2048 * _k, b_gs[1] + 2048 * (_k + 1))
    b_idb = carve("idb_s", 256); reg_iv(b_idb, 256)
    b_pm = carve("pm_s", 32); reg_iv(b_pm, 32)
    b_ones = carve("ones_s", 1024); reg_iv(b_ones, 1024)
    b_eps = carve("eps_s", 4); reg_iv(b_eps, 4)
    b_nbig = carve("nbig_s", 4); reg_iv(b_nbig, 4)
    b_tiny = carve("tiny_s", 4); reg_iv(b_tiny, 4)
    b_iotmb = carve("iotmb_s", 1024); reg_iv(b_iotmb, 1024)
    b_bqc = carve("bqc_s", 32); reg_iv(b_bqc, 32)
    b_bkc = carve("bkc_s", 32); reg_iv(b_bkc, 32)
    b_gw = carve("gw", 256); reg_iv(b_gw, 256)
    b_ebias = carve("ebias_s", 64); reg_iv(b_ebias, 64)

    AOFF = (alloc_off[0] + 63) // 64 * 64
    U = lambda off: AOFF + off

    def ubuf(name, off, nbytes):
        b = (name, U(off))
        p.interval(name, "sbuf", U(off), U(off) + nbytes)
        return b

    b_af = ubuf("af", 0, 32768)
    b_Kf8 = [ubuf("Kf8_%d" % _k, 32768 + 4096 * _k, 4096) for _k in range(8)]
    b_Vaug8 = [ubuf("Va8_%d" % _k, 65536 + 4160 * _k, 4160) for _k in range(8)]
    b_wring = (None, U(98816))
    for _k in range(4):
        p.interval("wr%d" % _k, "sbuf", U(98816) + 2048 * _k, U(98816) + 2048 * (_k + 1))
    b_bring = (None, U(107008))
    for _k in range(4):
        p.interval("br%d" % _k, "sbuf", U(107008) + 256 * _k, U(107008) + 256 * (_k + 1))
    b_Qf = ubuf("Qf", 108032, 4096)
    b_Qm = ubuf("Qm", 112128, 4096)
    b_scanA = ("scanA", U(108032))
    p.interval("scanA", "sbuf", U(108032), U(108032) + 2210)
    b_scanB = ("scanB", U(120320))
    p.interval("scanB", "sbuf", U(120320), U(120320) + 2080)
    b_Ktok = ubuf("Ktok", 116224, 4096)
    b_SlocS = ubuf("SlocS", 120320, 2080)
    p.interval("bvr_s", "sbuf", U(122400), U(122400) + 2048)
    b_SlocR = (None, U(122400))
    for _k in range(2):
        p.interval("SlR%d" % _k, "sbuf", U(122400) + 2080 * _k, U(122400) + 2080 * (_k + 1))
    b_Saug = (None, U(126560))
    for _k in range(16):
        p.interval("Sg%d" % _k, "sbuf", U(126560) + 130 * _k, U(126560) + 130 * (_k + 1))
    b_Sfp = ubuf("Sfp", 128640, 260)
    b_Mp = (None, U(128960))
    for _k in range(2):
        p.interval("Mp%d" % _k, "sbuf", U(128960) + 1024 * _k, U(128960) + 1024 * (_k + 1))
    b_otb = (None, U(131008))
    for _k in range(2):
        p.interval("otb%d" % _k, "sbuf", U(131008) + 512 * _k, U(131008) + 512 * (_k + 1))
    b_rden = (None, U(132032))
    for _k in range(2):
        p.interval("rden%d" % _k, "sbuf", U(132032) + 16 * _k, U(132032) + 16 * (_k + 1))
    b_Sprep = ubuf("Sprep", 132096, 2080)
    b_Sprepb = ubuf("Sprepb", 134176, 1040)
    b_gath = ubuf("gath", 135232, 2080)
    b_mask2 = ubuf("mask2_s", 137344, 2048)
    UNEED_ATTN = 139392

    b_er = [ubuf("er%d" % _k, 8192 * _k, 8192) for _k in range(6)]
    b_wo = ubuf("wo", 32768, 16384)
    b_act = (None, U(49152))
    for _ic in range(4):
        for _u in range(2):
            p.interval("act%d_%d" % (_ic, _u), "sbuf",
                       U(49152) + 2048 * _ic + 1024 * _u,
                       U(49152) + 2048 * _ic + 1024 * (_u + 1))
    b_yv = ubuf("yv", 57344, 32768)
    b_xt32 = (None, U(90112))
    for _k in range(2):
        p.interval("xt%d" % _k, "sbuf", U(90112) + 4096 * _k, U(90112) + 4096 * (_k + 1))
    b_xsq = (None, U(98304))
    for _k in range(2):
        p.interval("xsq%d" % _k, "sbuf", U(98304) + 4096 * _k, U(98304) + 4096 * (_k + 1))
    b_xsqB = ubuf("xsqB", 106496, 4096)
    b_scores = ubuf("scores", 110592, 1024)
    b_comb = ubuf("comb", 111616, 1024)
    b_tmpr = ubuf("tmpr", 112640, 1024)
    b_eq = ubuf("eqb", 113664, 1024)
    b_gall = ubuf("gall", 114688, 256)
    b_tau = ubuf("tau", 114944, 64)
    b_gm = ubuf("gm", 115008, 256)
    b_rsc = ubuf("rsc", 115264, 128)
    b_mst = ubuf("mst", 115392, 768)
    b_bo = ubuf("bo_s", 116160, 2048)
    UNEED = max(UNEED_ATTN, 118208)
    alloc_off[0] = AOFF + UNEED
    assert alloc_off[0] <= SBUF_BYTES, alloc_off[0]

    xf = view(b_xf, BF16, 16384)
    xbf = view(b_xbf, BF16, 16384)
    af = view(b_af, BF16, 16384)
    yv = view(b_yv, F32, 8192)
    wo_s = view(b_wo, BF16, 8192)
    Qf = view(b_Qf, BF16, 2048)
    Qm = view(b_Qm, BF16, 2048)
    scanA = view(b_scanA, BF16, 1105)
    scanB = view((None, U(120320)), BF16, 1040)
    Kf8 = [view(b_Kf8[_k], BF16, 2048) for _k in range(8)]
    Vaug8 = [view(b_Vaug8[_k], BF16, 2080) for _k in range(8)]
    Ktok = view(b_Ktok, BF16, 2048)
    SlocS = view(b_SlocS, BF16, 1040)
    SlocR = [view((None, U(122400) + 2080 * _k), BF16, 1040) for _k in range(2)]
    Mp2 = [view((None, U(128960) + 1024 * _k), BF16, 512) for _k in range(2)]
    otb2 = [view((None, U(131008) + 512 * _k), BF16, 256) for _k in range(2)]
    Saug = view((None, U(126560)), BF16, 1040)
    Sfp = view(b_Sfp, F32, 65)
    rden2 = [view((None, U(132032) + 16 * _k), F32, 4) for _k in range(2)]
    wring = [view((None, U(98816) + 2048 * _k), BF16, 1024) for _k in range(4)]
    bring = [view((None, U(107008) + 256 * _k), BF16, 128, part=1) for _k in range(4)]
    gs2 = [view((b_gs[0], b_gs[1] + 2048 * _k), F32, 512) for _k in range(2)]
    ers = [view(b_er[_k], BF16, 4096) for _k in range(6)]
    actb = view(b_act, BF16, 4096)
    Sprep = view(b_Sprep, F32, 520)
    Sprepb = view(b_Sprepb, BF16, 520)
    gath = view(b_gath, F32, 520)
    scores = view(b_scores, F32, 256)
    comb = view(b_comb, F32, 256)
    tmpr = view(b_tmpr, F32, 256)
    eqb = view(b_eq, F32, 256)
    gall = view(b_gall, F32, 64)
    tau = view(b_tau, F32, 16)
    gm = view(b_gm, F32, 64)
    rsc = view(b_rsc, F32, 32)
    mst = view(b_mst, F32, 192)
    xt32 = [view((None, U(90112) + 4096 * _k), F32, 1024) for _k in range(2)]
    xsq2 = [view((None, U(98304) + 4096 * _k), F32, 1024) for _k in range(2)]
    xsqB = view(b_xsqB, F32, 1024)
    gw_s = view(b_gw, BF16, 128)
    bo_s = view(b_bo, BF16, 1024, part=1)
    idb = view(b_idb, BF16, 128)
    mask2 = view(b_mask2, F32, 512)
    iotmb = view(b_iotmb, F32, 256)
    pm_s = view(b_pm, F32, 8)
    ones_bf = view(b_ones, BF16, 512, part=1)
    eps_s = view(b_eps, F32, 1)
    nbig_s = view(b_nbig, F32, 1)
    tiny_s = view(b_tiny, F32, 1)
    ebias_s = view(b_ebias, F32, 16)
    bqc_s = view(b_bqc, F32, 8)
    bkc_s = view(b_bkc, F32, 8)
    bvr_s = view((None, U(122400)), BF16, 1024, part=1)

    pb = [es.enter_context(nc.psum_tensor("pbank%d" % _k, [128, 512], F32)) for _k in range(8)]
    for k in range(8):
        p.interval("pb%d" % k, "psum", 2048 * k, 2048 * (k + 1))

    def pbf(k):
        return pb[k][:]

    def pbb(k):
        return pb[k][:].bitcast(BF16)

    E = p.emit
    DMA = p.dma

    def EG(eng, fns, reads=(), writes=()):
        def run(e, fns=tuple(fns)):
            inst = None
            for f in fns:
                inst = f(e)
            return inst
        E(eng, run, reads=reads, writes=writes)

    def _finalize():
        sem_names = sorted({s for e in ENGINES for item in p.stream[e]
                            for s in [item[1] if item[0] == "wait" else item[2]]})
        handles = {}
        for s in sem_names:
            handles[s] = es.enter_context(nc.semaphore(s))
        qfinal = {s: v for s, v in p.cnt.items() if s.startswith("q") and not s.startswith("dmaidx")}
        with nc.Block() as block:
            def run_stream(engine, name):
                cur = None
                cur_id = None
                for item in p.stream[name]:
                    if item[0] == "wait":
                        engine.wait_ge(handles[item[1]], item[2])
                    else:
                        _, fn, sem, amount, scope = item
                        if scope != cur:
                            if cur is not None:
                                nc.leave_named_scope(cur, cur_id, False)
                            cur_id, _ = nc.enter_named_scope(scope, False)
                            cur = scope
                        inst = fn(engine)
                        inst.then_inc(handles[sem], amount)
                if cur is not None:
                    nc.leave_named_scope(cur, cur_id, False)
                if name == "sync":
                    for s, v in sorted(qfinal.items()):
                        engine.wait_ge(handles[s], v)

            @block.sync
            def _(e):
                run_stream(e, "sync")

            @block.gpsimd
            def _(e):
                run_stream(e, "gpsimd")

            @block.tensor
            def _(e):
                run_stream(e, "tensor")

            @block.vector
            def _(e):
                run_stream(e, "vector")

            @block.scalar
            def _(e):
                run_stream(e, "scalar")
        return nc, es, p

    p.scope = "load"
    DMA("sync", lambda e: e.dma_start(out=idb, in_=idb_d[:]), writes=["idb_s"])
    DMA("sync", lambda e: e.dma_start(out=mask2, in_=mask2_d[:]), writes=["mask2_s"])
    DMA("sync", lambda e: e.dma_start(out=iotmb, in_=iotmb_d[:]), writes=["iotmb_s"])
    DMA("sync", lambda e: e.dma_start(out=pm_s, in_=pmask_d[:]), writes=["pm_s"])
    DMA("sync", lambda e: e.dma_start(out=ebias_s, in_=ebias_d[:]), writes=["ebias_s"])
    DMA("sync", lambda e: e.dma_start(out=bqc_s, in_=bqc_d), writes=["bqc_s"])
    DMA("sync", lambda e: e.dma_start(out=bkc_s, in_=bkc_d), writes=["bkc_s"])
    DMA("gpsimd", lambda e: e.dma_start(out=bvr_s, in_=bv_d[:]), writes=["bvr_s"])
    DMA("gpsimd", lambda e: e.dma_start(out=xbf.rearrange("q (n h) -> q n h", h=D),
                                        in_=x_d.rearrange("(n q) h -> q n h", q=128)), writes=["xbf"])
    DMA("gpsimd", lambda e: e.dma_start(out=gw_s, in_=gwT_d), writes=["gw"])
    E("vector", lambda e: e.memset(ones_bf, 1.0), writes=["ones_s"])
    E("vector", lambda e: e.memset(eps_s, 1e-5), writes=["eps_s"])
    E("vector", lambda e: e.memset(nbig_s, -BIGV), writes=["nbig_s"])
    E("vector", lambda e: e.memset(tiny_s, 1e-20), writes=["tiny_s"])
    for pr in range(NPAIR):
        E("vector", lambda e, pr=pr: e.memset(
            Vaug8[pr].rearrange("q (j h c) -> q j h c", h=2, c=65)[:, :, :, 64:65], 1.0),
            writes=["Va8_%d" % pr])
    E("vector", lambda e: e.memset(Qm[0:64, :], 0.0), writes=["Qm"])
    E("vector", lambda e: e.memset(scanA[:, 0:65], 0.0), writes=["scanA"])

    def tile_transposes(n, src_reg):
        for c2 in range(4):
            k = 4 + ((n * 4 + c2) % 2)
            EG("tensor", [
                lambda e, n=n, c2=c2, k=k: e.transpose(
                    pbb(k)[:, 0:128], xbf[:, 1024 * n + 256 * c2: 1024 * n + 256 * c2 + 128], idb),
                lambda e, n=n, c2=c2, k=k: e.transpose(
                    pbb(k)[:, 128:256], xbf[:, 1024 * n + 256 * c2 + 128: 1024 * n + 256 * c2 + 256], idb)],
                reads=[src_reg, "idb_s"], writes=["pb%d" % k])
            E("vector", lambda e, n=n, c2=c2, k=k: e.tensor_copy(
                xf.rearrange("q (c r) -> q c r", r=2048)[:, 2 * c2:2 * c2 + 2, 128 * n:128 * (n + 1)],
                pbb(k)[:, 0:256].rearrange("q (a r) -> q a r", a=2)),
                reads=["pb%d" % k], writes=["xf"])

    p.scope = "xT"
    for n in range(NT):
        tile_transposes(n, "xbf")

    if debug:
        for n in range(16):
            DMA("sync", lambda e, n=n: e.dma_start(out=xf_dbg[:, 1024 * n:1024 * (n + 1)],
                                                   in_=xf[:, 1024 * n:1024 * (n + 1)]), reads=["xf"])
    if upto <= 1:
        return _finalize()

    wun = [0]

    def load_wslice(wd, pair):
        u = wun[0]; wun[0] += 1
        slot = u % 4
        DMA("gpsimd", lambda e, wd=wd, pair=pair, slot=slot: e.dma_start(
            out=wring[slot], in_=wd[pair]),
            writes=["wr%d" % slot])
        return slot

    def qkv_fmajor(slot, dst, dst_reg, bcol, breg, pair):
        for u in range(4):
            bank = u
            fns = [lambda e, u=u, c=c, slot=slot, bank=bank: e.matmul(
                pbf(bank), wring[slot].rearrange("q (c o) -> q c o", o=128)[:, c, :],
                xf[:, 2048 * c + 512 * u: 2048 * c + 512 * (u + 1)],
                start=(c == 0), stop=(c == 7)) for c in range(8)]
            EG("tensor", fns,
               reads=["wr%d" % slot, "xf"], writes=["pb%d" % bank])
            E("scalar", lambda e, u=u, bank=bank, bcol=bcol, pair=pair: e.activation(
                gs2[bank % 2], pbf(bank), AF.Exp, bias=bcol[:, pair:pair + 1]),
                reads=["pb%d" % bank, breg], writes=["gs%d" % (bank % 2)])
            E("scalar", lambda e, u=u, bank=bank, dst=dst, dst_reg=dst_reg: e.activation(
                dst[:, 512 * u:512 * (u + 1)], gs2[bank % 2], AF.Ln, bias=1.0),
                reads=["gs%d" % (bank % 2)], writes=[dst_reg])

    for pair in range(NPAIR):
        p.scope = "A%d" % pair
        sk = load_wslice(wkT_d, pair)
        sv = load_wslice(wvT_d, pair)
        qkv_fmajor(sk, Kf8[pair], "Kf8_%d" % pair, bkc_s, "bkc_s", pair)
        for jj in range(NCH // 2):
            k = 4 + (jj % 4)
            EG("tensor", [
                lambda e, jj=jj, k=k, pair=pair: e.transpose(
                    pbb(k)[:, 0:128], Kf8[pair][:, 256 * jj:256 * jj + 128], idb),
                lambda e, jj=jj, k=k, pair=pair: e.transpose(
                    pbb(k)[:, 128:256], Kf8[pair][:, 256 * jj + 128:256 * jj + 256], idb)],
                reads=["Kf8_%d" % pair, "idb_s"], writes=["pb%d" % k])
            E("vector", lambda e, jj=jj, k=k: e.tensor_copy(
                Ktok[:, 256 * jj:256 * (jj + 1)], pbb(k)[:, 0:256]),
                reads=["pb%d" % k], writes=["Ktok"])
        for jj in range(NCH // 2):
            bank = jj % 4
            fns = []
            for sub in range(2):
                j = 2 * jj + sub
                fns += [lambda e, j=j, c=c, sub=sub, slot=sv, bank=bank: e.matmul(
                    pbf(bank)[:, 128 * sub:128 * (sub + 1)],
                    xf[:, 2048 * c + 128 * j: 2048 * c + 128 * (j + 1)],
                    wring[slot].rearrange("q (c o) -> q c o", o=128)[:, c, :],
                    start=(c == 0), stop=False) for c in range(8)]
                fns.append(lambda e, sub=sub, pair=pair, bank=bank: e.matmul(
                    pbf(bank)[:, 128 * sub:128 * (sub + 1)],
                    ones_bf[:, 0:128], bvr_s[:, 128 * pair:128 * (pair + 1)],
                    start=False, stop=True))
            EG("tensor", fns,
               reads=["wr%d" % sv, "xf", "bvr_s", "ones_s"], writes=["pb%d" % bank])
            E("vector", lambda e, jj=jj, bank=bank, pair=pair: e.tensor_copy(
                Vaug8[pair].rearrange("q (j h c) -> q j h c", h=2, c=65)[:, 2 * jj:2 * jj + 2, :, 0:64],
                pbf(bank)[:, 0:256].rearrange("q (a h c) -> q a h c", a=2, h=2)),
                reads=["pb%d" % bank], writes=["Va8_%d" % pair])
        for jj in range(NCH // 2):
            bank = 4 + (jj % 4)
            fns = []
            for sub in range(2):
                j = 2 * jj + sub
                fns += [lambda e, j=j, sub=sub, half=half, bank=bank, pair=pair: e.matmul(
                    pbf(bank)[64 * half:64 * (half + 1), 136 * sub:136 * sub + 65],
                    Ktok[:, 128 * j + 64 * half: 128 * j + 64 * (half + 1)],
                    Vaug8[pair][:, 130 * j + 65 * half: 130 * j + 65 * (half + 1)],
                    start=True, stop=True) for half in range(2)]
            EG("tensor", fns, reads=["Ktok", "Va8_%d" % pair], writes=["pb%d" % bank])
            E("vector", lambda e, jj=jj, bank=bank: e.tensor_copy(
                scanA[:, 65 + 130 * jj:65 + 130 * (jj + 1)].rearrange("q (a c) -> q a c", a=2),
                pbf(bank)[:, 0:272].rearrange("q (a c) -> q a c", c=136)[:, :, 0:65]),
                reads=["pb%d" % bank], writes=["scanA"])
        EG("tensor", [lambda e, j=j, half=half, pair=pair: e.matmul(
            pbf(0)[64 * half:64 * (half + 1), 0:65],
            Ktok[:, 128 * j + 64 * half: 128 * j + 64 * (half + 1)],
            Vaug8[pair][:, 130 * j + 65 * half: 130 * j + 65 * (half + 1)],
            start=(j == 0), stop=(j == NCH - 1))
            for half in range(2) for j in range(NCH)],
            reads=["Ktok", "Va8_%d" % pair], writes=["pb0"])
        E("vector", lambda e: e.tensor_copy(Sfp, pbf(0)[:, 0:65]),
          reads=["pb0"], writes=["Sfp"])
        a3 = lambda ap, nb: ap.rearrange("q (j c) -> q j c", c=65)
        for kk in range(4):
            step = 1 << kk
            E("vector", lambda e, step=step: e.tensor_copy(
                scanB[:, 0:65 * step], scanA[:, 65:65 * (step + 1)]),
                reads=["scanA"], writes=["scanB"])
            E("vector", lambda e, step=step: e.tensor_tensor(
                scanB[:, 65 * step:1040], scanA[:, 65 * (step + 1):1105],
                scanA[:, 65:65 * (17 - step)], ALU.add),
                reads=["scanA"], writes=["scanB"])
            E("vector", lambda e: e.tensor_copy(scanA[:, 65:1105], scanB),
              reads=["scanB"], writes=["scanA"])
        DMA("sync", lambda e, pair=pair: e.dma_start(out=sloc_d[pair], in_=scanA[:, 0:1040]),
            reads=["scanA"], writes=["sloc_hbm"])
        for half in range(2):
            DMA("sync", lambda e, pair=pair, half=half: e.dma_start(
                out=cc_in.ap()[:, 65 * (2 * pair + half): 65 * (2 * pair + half + 1)],
                in_=view(b_Sfp, F32, 65, part=64, poff=64 * half)),
                reads=["Sfp"], writes=["cc_in"])

    if debug:
        DMA("sync", lambda e: e.dma_start(out=ccin_dbg[:], in_=cc_in.ap()[:]), reads=["cc_in"])
    if upto <= 2:
        return _finalize()

    qslot = {}

    def phaseC_load(pair):
        qslot[pair] = load_wslice(wqT_d, pair)

    def phaseC_part1(pair):
        p.scope = "C1_%d" % pair
        if pair not in qslot:
            phaseC_load(pair)
        sq_ = qslot[pair]
        qkv_fmajor(sq_, Qf, "Qf", bqc_s, "bqc_s", pair)
        E("vector", lambda e: e.tensor_copy(Qm[64:128, :], Qf[64:128, :]),
          reads=["Qf"], writes=["Qm"])

    def phaseC_part2(pair):
        p.scope = "C2_%d" % pair
        Kfp = Kf8[pair]
        Vap = Vaug8[pair]
        rKf = "Kf8_%d" % pair
        rVa = "Va8_%d" % pair
        slr = "SlR%d" % (pair % 2)
        if pair + 1 < NPAIR:
            DMA("sync", lambda e, pair=pair: e.dma_start(
                out=SlocR[(pair + 1) % 2], in_=sloc_d[pair + 1]),
                reads=["sloc_hbm"], writes=["SlR%d" % ((pair + 1) % 2)])
        E("vector", lambda e, pair=pair, slr=slr: e.tensor_tensor(
            Saug.rearrange("q (j c) -> q j c", c=65),
            SlocR[pair % 2].rearrange("q (j c) -> q j c", c=65),
            Sprepb[:, 65 * pair:65 * (pair + 1)].rearrange("q (o c) -> q o c", o=1).broadcast_to((128, 16, 65)),
            ALU.add),
            reads=[slr, "Sprepb"], writes=["Sg%d" % _j for _j in range(16)])

        def stage1(uu):
            ab = uu % 2
            fns = []
            for sub in range(2):
                j = 2 * uu + sub
                fns += [
                    lambda e, j=j, sub=sub, ab=ab: e.matmul(
                        pbf(ab)[:, 256 * sub:256 * sub + 128],
                        Kfp[0:64, 128 * j:128 * (j + 1)],
                        Qf[0:64, 128 * j:128 * (j + 1)],
                        start=True, stop=True),
                    lambda e, j=j, sub=sub, ab=ab: e.matmul(
                        pbf(ab)[:, 256 * sub + 128:256 * sub + 256],
                        Kfp[:, 128 * j:128 * (j + 1)],
                        Qm[:, 128 * j:128 * (j + 1)],
                        start=True, stop=True)]
            EG("tensor", fns, reads=[rKf, "Qf", "Qm"], writes=["pb%d" % ab])
            E("vector", lambda e, uu=uu, ab=ab: e.tensor_tensor(
                Mp2[uu % 2], pbf(ab)[:, 0:512], mask2, ALU.mult),
                reads=["pb%d" % ab, "mask2_s"], writes=["Mp%d" % (uu % 2)])

        def stage2(uu):
            nb = 2 + (uu % 2)
            fns = []
            for sub in range(2):
                j = 2 * uu + sub
                co0 = 136 * sub
                fns += [
                    lambda e, j=j, sub=sub, nb=nb, co0=co0: e.matmul(
                        pbf(nb)[:, co0:co0 + 65], Mp2[(j // 2) % 2][:, 256 * sub:256 * sub + 128],
                        Vap[:, 130 * j:130 * j + 65],
                        start=True, stop=False),
                    lambda e, j=j, nb=nb, co0=co0: e.matmul(
                        pbf(nb)[:, co0:co0 + 65], Qf[0:64, 128 * j:128 * (j + 1)],
                        Saug[0:64, 65 * j:65 * (j + 1)],
                        start=False, stop=True),
                    lambda e, j=j, sub=sub, nb=nb, co0=co0: e.matmul(
                        pbf(nb)[:, co0 + 68:co0 + 133], Mp2[(j // 2) % 2][:, 256 * sub + 128:256 * sub + 256],
                        Vap[:, 130 * j + 65:130 * j + 130],
                        start=True, stop=False),
                    lambda e, j=j, nb=nb, co0=co0: e.matmul(
                        pbf(nb)[:, co0 + 68:co0 + 133], Qm[:, 128 * j:128 * (j + 1)],
                        Saug[:, 65 * j:65 * (j + 1)],
                        start=False, stop=True)]
            EG("tensor", fns,
               reads=["Mp%d" % (uu % 2), rVa, "Qf", "Qm",
                      "Sg%d" % (2 * uu), "Sg%d" % (2 * uu + 1)],
               writes=["pb%d" % nb])
            E("vector", lambda e, uu=uu, nb=nb: e.reciprocal(
                rden2[uu % 2],
                pbf(nb)[:, 0:272].rearrange("q (h c) -> q h c", c=68)[:, :, 64]),
                reads=["pb%d" % nb], writes=["rden%d" % (uu % 2)])
            for sub in range(2):
                for half in range(2):
                    co = 136 * sub + 68 * half
                    E("scalar", lambda e, uu=uu, sub=sub, half=half, nb=nb, co=co: e.activation(
                        otb2[uu % 2][:, 128 * sub + 64 * half:128 * sub + 64 * (half + 1)],
                        pbf(nb)[:, co:co + 64], AF.Copy,
                        scale=rden2[uu % 2][:, 2 * sub + half:2 * sub + half + 1]),
                        reads=["pb%d" % nb, "rden%d" % (uu % 2)], writes=["otb%d" % (uu % 2)])

        def stage3(uu):
            tb = 4 + (uu % 2)
            EG("tensor", [
                lambda e, uu=uu, tb=tb: e.transpose(
                    pbb(tb)[:, 0:128], otb2[uu % 2][:, 0:128], idb),
                lambda e, uu=uu, tb=tb: e.transpose(
                    pbb(tb)[:, 128:256], otb2[uu % 2][:, 128:256], idb)],
                reads=["otb%d" % (uu % 2), "idb_s"], writes=["pb%d" % tb])
            E("vector", lambda e, uu=uu, pair=pair, tb=tb: e.tensor_copy(
                af[:, 2048 * pair + 256 * uu: 2048 * pair + 256 * (uu + 1)], pbb(tb)[:, 0:256]),
                reads=["pb%d" % tb], writes=["af"])

        NU = NCH // 2
        for i in range(NU + 2):
            if i < NU:
                stage1(i)
            if 0 <= i - 1 < NU:
                stage2(i - 1)
            if 0 <= i - 2 < NU:
                stage3(i - 2)

    if upto >= 4:
        phaseC_part1(0)
        DMA("sync", lambda e: e.dma_start(out=SlocR[0], in_=sloc_d[0]),
            reads=["sloc_hbm"], writes=["SlR0"])

    p.scope = "coll"
    E("gpsimd", lambda e: e.collective_compute(
        "AllGather", ALU.bypass, replica_groups=[list(range(NCORE))],
        ins=[cc_in.ap()], outs=[cc_out.ap()]),
        reads=["cc_in"], writes=["cc_out"], sem="cc", amount=1)
    E("vector", lambda e: e.memset(Sprep, 0.0), writes=["Sprep"])
    for r in range(NCORE):
        for t in range(2):
            DMA("sync", lambda e, r=r, t=t: e.dma_start(
                out=gath[64 * t:64 * (t + 1), :].rearrange("q (pp c) -> q pp c", c=65),
                in_=cc_out.ap()[64 * r:64 * (r + 1), :].rearrange(
                    "q (pp two c) -> q two pp c", two=2, c=65)[:, t]),
                reads=["cc_out"], writes=["gath"])
        E("vector", lambda e, r=r: e.scalar_tensor_tensor(
            Sprep, gath, pm_s[:, r:r + 1], Sprep, ALU.mult, ALU.add),
            reads=["gath", "pm_s", "Sprep"], writes=["Sprep"])
    E("scalar", lambda e: e.copy(Sprepb, Sprep), reads=["Sprep"], writes=["Sprepb"])

    if debug:
        DMA("sync", lambda e: e.dma_start(out=sprep_dbg[:], in_=Sprep), reads=["Sprep"])
    if upto <= 3:
        return _finalize()

    phaseC_load(1)
    phaseC_part2(0)
    for pair in range(1, NPAIR):
        phaseC_part1(pair)
        if pair + 1 < NPAIR:
            phaseC_load(pair + 1)
        phaseC_part2(pair)
        if pair == 3:
            p.scope = "C2_%d" % pair
            DMA("gpsimd", lambda e: e.dma_start(out=bo_s, in_=bo_d[:]), writes=["bo_s"])
            DMA("gpsimd", lambda e: e.dma_start(out=wo_s, in_=woT_d), writes=["wo"])

    if debug:
        for n in range(16):
            DMA("sync", lambda e, n=n: e.dma_start(out=af_dbg[:, 1024 * n:1024 * (n + 1)],
                                                   in_=af[:, 1024 * n:1024 * (n + 1)]), reads=["af"])
    if upto <= 4:
        return _finalize()

    p.scope = "D"
    for n in range(NT):
        par = n % 2
        for oh in range(2):
            bank = (n * 2 + oh) % 4
            fns = [lambda e, n=n, oh=oh, c=c, bank=bank: e.matmul(
                pbf(bank), af[:, 2048 * c + 128 * n:2048 * c + 128 * (n + 1)],
                wo_s.rearrange("q (c o) -> q c o", o=1024)[:, c, 512 * oh:512 * (oh + 1)],
                start=(c == 0), stop=False) for c in range(8)]
            fns.append(lambda e, oh=oh, bank=bank: e.matmul(
                pbf(bank), ones_bf[:, 0:128], bo_s[:, 512 * oh:512 * (oh + 1)],
                start=False, stop=True))
            EG("tensor", fns, reads=["af", "wo", "ones_s", "bo_s"], writes=["pb%d" % bank])
        for oh in range(2):
            bank = (n * 2 + oh) % 4
            E("vector", lambda e, n=n, oh=oh, bank=bank, par=par: e.scalar_tensor_tensor(
                xt32[par][:, 512 * oh:512 * (oh + 1)], pbf(bank),
                eps_s, xbf[:, 1024 * n + 512 * oh:1024 * n + 512 * (oh + 1)],
                ALU.bypass, ALU.add, accum_out=mst[:, 4 * n + oh:4 * n + oh + 1]),
                reads=["pb%d" % bank, "xbf", "eps_s"], writes=["xt%d" % par, "mst"])
        E("vector", lambda e, n=n: e.tensor_tensor(
            mst[:, 4 * n:4 * n + 1], mst[:, 4 * n:4 * n + 1], mst[:, 4 * n + 1:4 * n + 2], ALU.add),
            reads=["mst"], writes=["mst"])
        E("scalar", lambda e, n=n: e.mul(mst[:, 4 * n + 1:4 * n + 2], mst[:, 4 * n:4 * n + 1], -1.0 / D),
          reads=["mst"], writes=["mst"])
        E("scalar", lambda e, n=n, par=par: e.activation(
            xsq2[par], xt32[par], AF.Square,
            bias=mst[:, 4 * n + 1:4 * n + 2], accum_out=mst[:, 4 * n + 2:4 * n + 3]),
            reads=["xt%d" % par, "mst"], writes=["xsq%d" % par, "mst"])
        E("scalar", lambda e, n=n: e.activation(
            mst[:, 4 * n + 2:4 * n + 3], mst[:, 4 * n + 2:4 * n + 3], AF.Ln,
            scale=1.0 / D, bias=eps_s),
            reads=["mst", "eps_s"], writes=["mst"])
        E("scalar", lambda e, n=n: e.activation(
            mst[:, 4 * n + 3:4 * n + 4], mst[:, 4 * n + 2:4 * n + 3], AF.Exp, scale=-0.5),
            reads=["mst"], writes=["mst"])
        E("vector", lambda e, n=n, par=par: e.tensor_scalar(
            xbf[:, 1024 * n:1024 * (n + 1)], xt32[par],
            mst[:, 4 * n + 1:4 * n + 2], mst[:, 4 * n + 3:4 * n + 4], ALU.add, ALU.mult),
            reads=["xt%d" % par, "mst"], writes=["xbf"])
        tile_transposes(n, "xbf")

    if debug:
        for n in range(16):
            DMA("sync", lambda e, n=n: e.dma_start(out=x1_dbg[:, 1024 * n:1024 * (n + 1)],
                                                   in_=xbf[:, 1024 * n:1024 * (n + 1)]), reads=["xbf"])
    if upto <= 5:
        return _finalize()

    p.scope = "R"
    for n in range(NT):
        bank = 6 + (n % 2)
        EG("tensor", [lambda e, n=n, c=c, bank=bank: e.matmul(
            pbf(bank)[:, 0:16], xf[:, 2048 * c + 128 * n:2048 * c + 128 * (n + 1)],
            gw_s.rearrange("q (c e) -> q c e", e=16)[:, c, :],
            start=(c == 0), stop=(c == 7)) for c in range(8)],
            reads=["xf", "gw"], writes=["pb%d" % bank])
        E("scalar", lambda e, n=n, bank=bank: e.activation(scores[:, 16 * n:16 * (n + 1)], pbf(bank)[:, 0:16], AF.Sigmoid),
          reads=["pb%d" % bank], writes=["scores"])
    E("vector", lambda e: e.tensor_tensor(
        tmpr.rearrange("q (t e) -> q t e", e=16), scores.rearrange("q (t e) -> q t e", e=16),
        ebias_s.rearrange("q (o e) -> q o e", o=1).broadcast_to((128, 16, 16)),
        ALU.add), reads=["scores", "ebias_s"], writes=["tmpr"])
    pairs = [(0, 1), (0, 2), (0, 3), (1, 2), (1, 3), (2, 3)]
    for idx, (i, jj) in enumerate(pairs):
        if idx == 0:
            E("vector", lambda e, i=i, jj=jj: e.tensor_tensor(
                gall, tmpr[:, i::4], tmpr[:, jj::4], ALU.add), reads=["tmpr"], writes=["gall"])
        else:
            E("vector", lambda e, i=i, jj=jj: e.tensor_tensor(
                eqb[:, 0:64], tmpr[:, i::4], tmpr[:, jj::4], ALU.add), reads=["tmpr"], writes=["eqb"])
            E("vector", lambda e: e.tensor_tensor(gall, gall, eqb[:, 0:64], ALU.max),
              reads=["gall", "eqb"], writes=["gall"])
    for idx, (i, jj) in enumerate(pairs):
        if idx == 0:
            E("vector", lambda e, i=i, jj=jj: e.tensor_tensor(
                tau, gall[:, i::4], gall[:, jj::4], ALU.min), reads=["gall"], writes=["tau"])
        else:
            E("vector", lambda e, i=i, jj=jj: e.tensor_tensor(
                rsc[:, 0:16], gall[:, i::4], gall[:, jj::4], ALU.min), reads=["gall"], writes=["rsc"])
            E("vector", lambda e: e.tensor_tensor(tau, tau, rsc[:, 0:16], ALU.max),
              reads=["tau", "rsc"], writes=["tau"])
    for g in range(4):
        E("vector", lambda e, g=g: e.tensor_tensor(gm[:, g::4], gall[:, g::4], tau, ALU.is_ge),
          reads=["gall", "tau"], writes=["gm"])
    for i in range(4):
        E("vector", lambda e, i=i: e.tensor_tensor(tmpr[:, i::4], tmpr[:, i::4], gm, ALU.mult),
          reads=["tmpr", "gm"], writes=["tmpr"])
    E("vector", lambda e: e.memset(comb, 0.0), writes=["comb"])
    t3 = lambda ap: ap.rearrange("q (t e) -> q t e", e=16)
    for rnd in range(4):
        E("vector", lambda e: e.reduce_max(rsc[:, 16:32].rearrange("q (t e) -> q t e", e=1), t3(tmpr), axis=AX.X),
          reads=["tmpr"], writes=["rsc"])
        E("vector", lambda e: e.tensor_tensor(
            t3(eqb), t3(tmpr), rsc[:, 16:32].rearrange("q (t e) -> q t e", e=1).broadcast_to((128, 16, 16)), ALU.is_ge),
            reads=["tmpr", "rsc"], writes=["eqb"])
        E("vector", lambda e: e.tensor_tensor(eqb, eqb, iotmb, ALU.mult),
          reads=["eqb", "iotmb_s"], writes=["eqb"])
        E("vector", lambda e: e.tensor_reduce(
            rsc[:, 16:32].rearrange("q (t e) -> q t e", e=1), t3(eqb), AX.X, ALU.min),
            reads=["eqb"], writes=["rsc"])
        E("vector", lambda e: e.tensor_tensor(
            t3(eqb), t3(iotmb), rsc[:, 16:32].rearrange("q (t e) -> q t e", e=1).broadcast_to((128, 16, 16)), ALU.is_equal),
            reads=["iotmb_s", "rsc"], writes=["eqb"])
        E("vector", lambda e: e.tensor_tensor(comb, comb, eqb, ALU.add),
          reads=["comb", "eqb"], writes=["comb"])
        if rnd < 3:
            E("vector", lambda e: e.scalar_tensor_tensor(
                tmpr, eqb, nbig_s, tmpr, ALU.mult, ALU.add),
                reads=["eqb", "nbig_s", "tmpr"], writes=["tmpr"])
    E("vector", lambda e: e.tensor_tensor(comb, comb, scores, ALU.mult),
      reads=["comb", "scores"], writes=["comb"])
    E("vector", lambda e: e.reduce_sum(
        rsc[:, 16:32].rearrange("q (t e) -> q t e", e=1), t3(comb), axis=AX.X),
        reads=["comb"], writes=["rsc"])
    E("vector", lambda e: e.tensor_scalar(rsc[:, 16:32], rsc[:, 16:32], tiny_s, None, ALU.add),
      reads=["rsc", "tiny_s"], writes=["rsc"])
    E("vector", lambda e: e.reciprocal(rsc[:, 0:16], rsc[:, 16:32]),
      reads=["rsc"], writes=["rsc"])
    E("vector", lambda e: e.tensor_tensor(
        t3(comb), t3(comb), rsc[:, 0:16].rearrange("q (t e) -> q t e", e=1).broadcast_to((128, 16, 16)), ALU.mult),
        reads=["comb", "rsc"], writes=["comb"])

    if debug:
        DMA("sync", lambda e: e.dma_start(out=comb_dbg[:], in_=comb), reads=["comb"])
    if upto <= 6:
        return _finalize()

    eun = [0]

    def load_eunit(which, e_idx):
        u = eun[0]; eun[0] += 1
        slot = u % 6
        srcs = (gateT_d, upT_d, downT_d)[which]
        DMA("gpsimd", lambda e, srcs=srcs, e_idx=e_idx, slot=slot: e.dma_start(
            out=ers[slot], in_=srcs[e_idx]),
            writes=["er%d" % slot])
        return slot

    for hf in range(2):
        p.scope = "FFN%d" % hf
        for ei in range(NEXP):
            sgate = load_eunit(0, ei)
            sup = load_eunit(1, ei)
            sdown = load_eunit(2, ei)
            for ic in range(4):
                for u in range(2):
                    it = ic * 2 + u
                    gbank = it % 2
                    ubank = 2 + it % 2
                    gsp = gs2[it % 2]
                    EG("tensor", [lambda e, ic=ic, u=u, c=c, gbank=gbank, sgate=sgate, hf=hf: e.matmul(
                        pbf(gbank), ers[sgate].rearrange("q (c i) -> q c i", c=8)[:, c, 128 * ic:128 * (ic + 1)],
                        xf[:, 2048 * c + 1024 * hf + 512 * u: 2048 * c + 1024 * hf + 512 * (u + 1)],
                        start=(c == 0), stop=(c == 7)) for c in range(8)],
                        reads=["er%d" % sgate, "xf"], writes=["pb%d" % gbank])
                    EG("tensor", [lambda e, ic=ic, u=u, c=c, ubank=ubank, sup=sup, hf=hf: e.matmul(
                        pbf(ubank), ers[sup].rearrange("q (c i) -> q c i", c=8)[:, c, 128 * ic:128 * (ic + 1)],
                        xf[:, 2048 * c + 1024 * hf + 512 * u: 2048 * c + 1024 * hf + 512 * (u + 1)],
                        start=(c == 0), stop=(c == 7)) for c in range(8)],
                        reads=["er%d" % sup, "xf"], writes=["pb%d" % ubank])
                    E("scalar", lambda e, gbank=gbank, gsp=gsp: e.activation(gsp, pbf(gbank), AF.Silu),
                      reads=["pb%d" % gbank], writes=["gs%d" % (it % 2)])
                    E("vector", lambda e, ic=ic, u=u, ubank=ubank, gsp=gsp: e.tensor_tensor(
                        actb[:, 1024 * ic + 512 * u: 1024 * ic + 512 * (u + 1)], gsp, pbf(ubank), ALU.mult),
                        reads=["gs%d" % (it % 2), "pb%d" % ubank], writes=["act%d_%d" % (ic, u)])
            for j in range(8):
                nglob = 8 * hf + j
                for oh in range(2):
                    bank = 4 + (j * 2 + oh) % 4
                    EG("tensor", [lambda e, j=j, oh=oh, ic=ic, bank=bank, sdown=sdown: e.matmul(
                        pbf(bank), actb[:, 1024 * ic + 128 * j: 1024 * ic + 128 * (j + 1)],
                        ers[sdown].rearrange("q (c o) -> q c o", c=4)[:, ic, 512 * oh:512 * (oh + 1)],
                        start=(ic == 0), stop=(ic == 3)) for ic in range(4)],
                        reads=["act%d_%d" % (ic2, j // 4) for ic2 in range(4)] + ["er%d" % sdown],
                        writes=["pb%d" % bank])
                    if ei == 0:
                        E("vector", lambda e, j=j, oh=oh, bank=bank, nglob=nglob: e.tensor_scalar(
                            yv[:, 1024 * j + 512 * oh:1024 * j + 512 * (oh + 1)], pbf(bank),
                            comb[:, 16 * nglob + 0:16 * nglob + 1], None, ALU.mult),
                            reads=["pb%d" % bank, "comb"], writes=["yv"])
                    else:
                        E("vector", lambda e, j=j, oh=oh, bank=bank, nglob=nglob, ei=ei: e.scalar_tensor_tensor(
                            yv[:, 1024 * j + 512 * oh:1024 * j + 512 * (oh + 1)], pbf(bank),
                            comb[:, 16 * nglob + ei:16 * nglob + ei + 1],
                            yv[:, 1024 * j + 512 * oh:1024 * j + 512 * (oh + 1)], ALU.mult, ALU.add),
                            reads=["pb%d" % bank, "comb", "yv"], writes=["yv"])
        p.scope = "LN2_%d" % hf
        for j in range(8):
            n = 8 * hf + j
            mb = 64 + 4 * n
            par = n % 2
            E("vector", lambda e, j=j, n=n, mb=mb, par=par: e.scalar_tensor_tensor(
                xsq2[par], yv[:, 1024 * j:1024 * (j + 1)], eps_s,
                xbf[:, 1024 * n:1024 * (n + 1)], ALU.bypass, ALU.add,
                accum_out=mst[:, mb:mb + 1]),
                reads=["yv", "xbf", "eps_s"], writes=["xsq%d" % par, "mst"])
            E("scalar", lambda e, mb=mb: e.mul(mst[:, mb + 1:mb + 2], mst[:, mb:mb + 1], -1.0 / D),
              reads=["mst"], writes=["mst"])
            E("scalar", lambda e, mb=mb, par=par: e.activation(
                xsqB[:, 0:1024], xsq2[par], AF.Square, bias=mst[:, mb + 1:mb + 2], accum_out=mst[:, mb + 2:mb + 3]),
                reads=["xsq%d" % par, "mst"], writes=["xsqB", "mst"])
            E("scalar", lambda e, mb=mb: e.activation(
                mst[:, mb + 2:mb + 3], mst[:, mb + 2:mb + 3], AF.Ln, scale=1.0 / D, bias=eps_s),
                reads=["mst", "eps_s"], writes=["mst"])
            E("scalar", lambda e, mb=mb: e.activation(mst[:, mb + 3:mb + 4], mst[:, mb + 2:mb + 3], AF.Exp, scale=-0.5),
              reads=["mst"], writes=["mst"])
            E("vector", lambda e, mb=mb, par=par: e.tensor_scalar(
                xsqB[:, 0:1024], xsq2[par], mst[:, mb + 1:mb + 2], mst[:, mb + 3:mb + 4], ALU.add, ALU.mult),
                reads=["xsq%d" % par, "mst"], writes=["xsqB"])
            DMA("sync", lambda e, n=n: e.dma_start(
                out=out_d[128 * n:128 * (n + 1), :], in_=xsqB[:, 0:1024]),
                reads=["xsqB"], writes=["outd"])

    return _finalize()


_CACHE = {}


def make_in_maps(inputs):
    x = np.asarray(inputs["L0_attn_x"], dtype=np.float32).reshape(S, D)
    wq = np.asarray(inputs["L0_attn_Wq"], dtype=np.float32)
    wk = np.asarray(inputs["L0_attn_Wk"], dtype=np.float32)
    wv = np.asarray(inputs["L0_attn_Wv"], dtype=np.float32)
    wo = np.asarray(inputs["L0_attn_Wo"], dtype=np.float32)
    bqc = np.ascontiguousarray(np.asarray(inputs["L0_attn_bq"], dtype=np.float32).reshape(8, 128).T)
    bkc = np.ascontiguousarray(np.asarray(inputs["L0_attn_bk"], dtype=np.float32).reshape(8, 128).T)
    bv = np.asarray(inputs["L0_attn_bv"], dtype=np.float32).reshape(1, D)
    bo = np.asarray(inputs["L0_attn_bo"], dtype=np.float32).reshape(1, D)
    gw = np.asarray(inputs["L0_ffn_gate_weight"], dtype=np.float32)
    ebias = np.asarray(inputs["L0_ffn_e_bias"], dtype=np.float32)
    gate = np.asarray(inputs["L0_ffn_gate_proj"], dtype=np.float32)
    up = np.asarray(inputs["L0_ffn_up_proj"], dtype=np.float32)
    down = np.asarray(inputs["L0_ffn_down_proj"], dtype=np.float32)

    def _sw_qkv(w):
        return np.ascontiguousarray(w.T.reshape(8, 128, 8, 128).transpose(2, 1, 0, 3).reshape(8, 128, 1024))
    wqT = _sw_qkv(wq)
    wkT = _sw_qkv(wk)
    wvT = _sw_qkv(wv)
    woT = np.ascontiguousarray(wo.T.reshape(8, 128, 1024).transpose(1, 0, 2).reshape(128, 8192))
    gwT = np.ascontiguousarray(gw.T.reshape(8, 128, 16).transpose(1, 0, 2).reshape(128, 128))
    gateT = np.ascontiguousarray(gate.transpose(0, 2, 1).reshape(16, 8, 128, 512).transpose(0, 2, 1, 3).reshape(16, 128, 4096))
    upT = np.ascontiguousarray(up.transpose(0, 2, 1).reshape(16, 8, 128, 512).transpose(0, 2, 1, 3).reshape(16, 128, 4096))
    downT = np.ascontiguousarray(down.transpose(0, 2, 1).reshape(16, 4, 128, 1024).transpose(0, 2, 1, 3).reshape(16, 128, 4096))
    ebias_t = np.ascontiguousarray(np.broadcast_to(ebias.reshape(1, NEXP), (128, NEXP)))
    idb = np.eye(128, dtype=ml_dtypes.bfloat16)
    m = np.triu(np.ones((128, 128), np.float32))
    mask2 = np.concatenate([m, m, m, m], axis=1)
    iotmb = np.ascontiguousarray(np.broadcast_to(np.tile(np.arange(16, dtype=np.float32), 16)[None, :], (128, 256)) - BIGV)

    in_maps = []
    for c in range(NCORE):
        pmask = np.zeros((128, NCORE), np.float32)
        pmask[:, :c] = 1.0
        in_maps.append({
            "x": np.ascontiguousarray(x[TC * c:TC * (c + 1)]),
            "wqT": wqT, "wkT": wkT, "wvT": wvT, "woT": woT,
            "bqc": bqc, "bkc": bkc, "bv": bv, "bo": bo,
            "gwT": gwT, "ebias": ebias_t,
            "gateT": gateT, "upT": upT, "downT": downT,
            "idb": idb, "mask2": mask2,
            "iotmb": iotmb, "pmask": pmask,
        })
    return in_maps


def run_probe(inputs, upto=99, debug=True, trace=False):
    nc, es, prog = build_program(upto=upto, debug=debug)
    print("probe upto=%d: n_ops=%d n_waits=%d" % (upto, prog.n_ops, prog.n_waits))
    in_maps = make_in_maps(inputs)
    res = run_bass_kernel_spmd(nc, in_maps, core_ids=list(range(NCORE)), trace=trace)
    return res


def kernel(**inputs):
    if "nc" not in _CACHE:
        nc, es, prog = build_program()
        _CACHE["nc"] = nc
        _CACHE["es"] = es
    nc = _CACHE["nc"]
    in_maps = make_in_maps(inputs)
    res = run_bass_kernel_spmd(nc, in_maps, core_ids=list(range(NCORE)))
    out = np.concatenate([np.asarray(r["out"], dtype=np.float32) for r in res.results], axis=0)
    return out.reshape(1, S, D)
